# version 03
# baseline (speedup 1.0000x reference)
import numpy as np
import ml_dtypes
import concourse.bass as bass
import concourse.mybir as mybir
from concourse.bass_utils import run_bass_kernel_spmd
from contextlib import ExitStack

F32 = mybir.dt.float32
BF16 = mybir.dt.bfloat16
F16 = mybir.dt.float16
I32 = mybir.dt.int32
I16 = mybir.dt.int16
ALU = mybir.AluOpType
AF = mybir.ActivationFunctionType

P = 128
H, HKV, DH, D = 16, 4, 128, 2048
T = 2048
CH = 128
NCHK = 16
NKD = 16
IDIM = 5632
NIC = 44
TT = 512
NTT = 4
NG = 4

LAM = np.exp2(-8.0 * np.arange(1, H + 1) / H)
BLK = [float(np.exp(-LAM[hh] * CH)) for hh in range(H)]
_SIGN_SH = [12, 8, 4, 0]
_MAG_SH = [6, 2, -2, -6]
ENGS = ("sync", "scalar", "vector", "tensor", "gpsimd")


def _perm_d():
    p = np.zeros(D, dtype=np.int64)
    for k in range(NKD):
        j, cc = k // 2, k % 2
        pp = np.arange(P)
        p[k * P:(k + 1) * P] = 8 * (cc * P + pp) + j
    return p


def _perm_i():
    p = np.zeros(IDIM, dtype=np.int64)
    for t in range(40):
        j2, cc2 = t // 5, t % 5
        pp = np.arange(P)
        p[t * P:(t + 1) * P] = 8 * (cc2 * P + pp) + j2
    for m in range(4):
        t = 40 + m
        for pp in range(P):
            if pp < 64:
                p[t * P + pp] = 8 * (640 + pp) + 2 * m
            else:
                p[t * P + pp] = 8 * (640 + pp - 64) + 2 * m + 1
    return p


def _gu_block(k):
    j, cc = k // 2, k % 2
    return cc * 4 + (j % 4), j // 4


def _dn_block(t):
    j2, cc2 = t // 5, t % 5
    return cc2 * 4 + (j2 % 4), j2 // 4


def _dn_pair_block(j2):
    return 20 + (j2 % 4), j2 // 4


class Sem:
    def __init__(self, h):
        self.h = h
        self.n = 0


class Prog:
    def __init__(self):
        self.streams = {e: [] for e in ENGS}
        self._lw = {}

    def em(self, eng, fn, desc=""):
        self.streams[eng].append(fn)
        if not hasattr(self, "log"):
            self.log = []
        self.log.append((eng, desc or getattr(fn, "_desc", "")))

    def wait(self, eng, sem, val=None):
        v = sem.n if val is None else val
        key = (eng, id(sem))
        if v <= 0 or self._lw.get(key, -1) >= v:
            return
        self._lw[key] = v
        h = sem.h
        self.em(eng, lambda e: e.wait_ge(h, v))

    def run(self, eng_obj, name):
        for fn in self.streams[name]:
            fn(eng_obj)


def build_nc(debug=False, phases=("dec", "a", "p", "b", "op", "c", "d")):
    nc = bass.Bass("TRN2", target_bir_lowering=False)

    def din(name, shape, dt):
        return nc.dram_tensor(name, shape, dt, kind="ExternalInput").ap()

    xT = din("xT", [D, T], F32)
    xpT = din("xpT", [D, T], F32)
    wqT = din("wqT", [D, H * DH], F32)
    wkT = din("wkT", [D, HKV * DH], F32)
    wvT = din("wvT", [D, HKV * DH], F32)
    woT = din("woT", [H * DH, D], F32)
    gupk = din("gupk", [256, 11264], I32)
    gusc = din("gusc", [256, 22528], F16)
    dnpk = din("dnpk", [704, 2048], I32)
    dnsc = din("dnsc", [704, 4096], F16)
    cosf = din("cosf", [P, T], BF16)
    sinf = din("sinf", [P, T], BF16)
    cospf = din("cospf", [P, T], BF16)
    sinpf = din("sinpf", [P, T], BF16)
    diagT = din("diagT", [P, H * CH], BF16)
    qdecT = din("qdecT", [P, H * CH], BF16)
    kdec = din("kdec", [P, H], F32)
    kdecF = din("kdecF", [P, H * NCHK], F32)
    rotm = din("rotm", [P, P], BF16)
    ident = din("ident", [P, P], BF16)
    onesb = din("onesb", [P, 1], BF16)
    onesf = din("onesf", [1, P], F32)
    consts = din("consts", [P, 4], F32)

    okind = "ExternalOutput" if debug else "Internal"
    outT = nc.dram_tensor("outT", [D, T], F32, kind="ExternalOutput").ap()
    wdec_gu = nc.dram_tensor("wdec_gu", [8 * P, 22528], BF16,
                             kind="Internal").ap()
    wdec_dn = nc.dram_tensor("wdec_dn", [24 * P, 4096], BF16,
                             kind="Internal").ap()
    o_bnc = nc.dram_tensor("o_bnc", [H * DH, T], BF16, kind=okind).ap()
    at_bnc = nc.dram_tensor("at_bnc", [D, T], BF16, kind=okind).ap()
    y_bnc = nc.dram_tensor("y_bnc", [D, T], BF16, kind=okind).ap()
    xn_dbg = (nc.dram_tensor("xn_dbg", [D, T], BF16,
                             kind="ExternalOutput").ap() if debug else None)
    kv_dbg = (nc.dram_tensor("kv_dbg", [P, H * DH], F32,
                             kind="ExternalOutput").ap() if debug else None)
    rows_dbg = (nc.dram_tensor("rows_dbg", [1, 3072], F32,
                               kind="ExternalOutput").ap() if debug else None)
    qg_dbg = (nc.dram_tensor("qg_dbg", [P, 8192], BF16,
                             kind="ExternalOutput").ap() if debug else None)
    wq_dbg = (nc.dram_tensor("wq_dbg", [P, 8192], BF16,
                             kind="ExternalOutput").ap() if debug else None)
    kv2_dbg = (nc.dram_tensor("kv2_dbg", [P, 4096 + 2048], BF16,
                              kind="ExternalOutput").ap() if debug else None)

    ctx = ExitStack()
    sbt = nc.alloc_sbuf_tensor
    XN = sbt("XN", [P, NKD * T], BF16)
    A1 = sbt("A1", [P, 8192], BF16)
    A2 = sbt("A2", [P, 8192], BF16)
    HB = sbt("HB", [P, NIC * TT], BF16)
    B1 = sbt("B1", [P, 8192], BF16)
    B2 = sbt("B2", [P, 4096], BF16)
    KV = sbt("KV", [P, H * DH], F32)
    KVB = sbt("KVB", [P, H * DH], BF16)
    SM = sbt("SM", [P, 2304], F32)
    ROWS = sbt("ROWS", [1, 3072], F32)
    MSQ = ROWS[0:1, 0:T]
    TB = {}
    for n, sh, dt in [
            ("diagT", [P, H * CH], BF16), ("qdecT", [P, H * CH], BF16),
            ("kdec", [P, H], F32), ("kdecF", [P, H * NCHK], F32),
            ("rotm", [P, P], BF16), ("ident", [P, P], BF16),
            ("onesb", [P, 1], BF16), ("onesf", [1, P], F32),
            ("consts", [P, 4], F32)]:
        TB[n] = sbt("t_" + n, sh, dt)

    PS = ctx.enter_context(nc.psum_tensor("PS", [P, 4096], F32))

    def bank(i, w=512):
        return PS[:, i * 512:i * 512 + w]

    def bank_bf(i):
        return PS[:, i * 512:(i + 1) * 512].bitcast(BF16)

    sm_t1 = SM[:, 0:512]
    sm_sq = [SM[:, 512:768].bitcast(BF16), SM[:, 768:1024].bitcast(BF16)]
    sm_ktmp = SM[:, 1024:1280].bitcast(BF16)
    sm_sT = [SM[:, 1280:1536].bitcast(BF16), SM[:, 1536:1792].bitcast(BF16)]
    sm_qp = [SM[:, 1792:2048].bitcast(BF16), SM[:, 2048:2304].bitcast(BF16)]
    sm_kd = [SM[:, 512:768].bitcast(BF16), SM[:, 768:1024].bitcast(BF16)]
    sm_ev = [SM[:, 1280:1536].bitcast(BF16), SM[:, 1536:1792].bitcast(BF16)]
    sm_mu = ROWS[0:1, 0:512]
    sm_ms = ROWS[0:1, 512:1024]
    sm_va = ROWS[0:1, 1024:1536]
    sm_ri = ROWS[0:1, 512:1024]
    d_va = ROWS[0:1, 2048:2560]
    d_ri = ROWS[0:1, 2560:3072]

    sems = {}
    for name in ["dLD", "dEV", "dST",
                 "aLD", "aPE", "aEV", "aSC", "aST",
                 "pLD", "pPE", "pEV",
                 "bLD", "bPE", "bEV", "bSC", "bST",
                 "oLD", "oPE", "oEV", "oST",
                 "cLD", "cPE", "cEV", "cSC", "cST",
                 "fLD", "fPE", "fEV", "fSC", "fST"]:
        sems[name] = Sem(ctx.enter_context(nc.semaphore(name)))
    S = sems
    pr = Prog()
    bguard = {}

    def OP(eng, make, sem=None, inc=1):
        if sem is not None:
            sem.n += inc
            h, v = sem.h, inc
            pr.em(eng, lambda e: make(e).then_inc(h, v))
            return sem.n
        pr.em(eng, lambda e: make(e))
        return None

    def guard_bank(i):
        if i in bguard:
            sem, val = bguard[i]
            pr.wait("tensor", sem, val)

    def set_guard(i, sem):
        bguard[i] = (sem, sem.n)

    def emit_tables():
        for n, src in [("diagT", diagT), ("qdecT", qdecT),
                       ("kdec", kdec), ("kdecF", kdecF), ("rotm", rotm),
                       ("ident", ident), ("onesb", onesb), ("onesf", onesf)]:
            OP("sync", lambda e, n=n, src=src: e.dma_start(
                out=TB[n][:], in_=src), S["aLD"], 16)

    def emit_decode():
        pk_t = A1[:].bitcast(I32)
        sc_raw = A2[:].bitcast(F16)
        sc_t = B1[:]
        wd_t = HB[:]

        def dec_one(n_i32, ld_pack, ld_scale, store):
            ld_pack()
            ld_scale()
            pr.wait("vector", S["dLD"])
            W = 2 * n_i32
            OP("vector", lambda e, W=W: e.tensor_scalar(
                sc_t[:, 0:W], sc_raw[:, 0:W], float(2.0 ** 126), None,
                ALU.mult))
            pk16 = pk_t[:, 0:n_i32].bitcast(I16)
            xn16 = XN[:].bitcast(I16)
            for jp in range(4):
                sa = xn16[:, 0:W]
                sb_ = xn16[:, 8192:8192 + W]
                ss, ms = _SIGN_SH[jp], _MAG_SH[jp]
                if ss > 0:
                    OP("vector", lambda e, sa=sa, pk16=pk16, ss=ss:
                       e.tensor_scalar(sa, pk16, ss, 0x8000 - 0x10000,
                                       ALU.logical_shift_left,
                                       ALU.bitwise_and))
                else:
                    OP("vector", lambda e, sa=sa, pk16=pk16:
                       e.tensor_scalar(sa, pk16, 0x8000 - 0x10000, None,
                                       ALU.bitwise_and))
                if ms > 0:
                    OP("vector", lambda e, sb_=sb_, pk16=pk16, ms=ms:
                       e.tensor_scalar(sb_, pk16, ms, 0x01C0,
                                       ALU.logical_shift_left,
                                       ALU.bitwise_and))
                else:
                    OP("vector", lambda e, sb_=sb_, pk16=pk16, ms=ms:
                       e.tensor_scalar(sb_, pk16, -ms, 0x01C0,
                                       ALU.logical_shift_right,
                                       ALU.bitwise_and))
                wdj = wd_t[:, jp * W:(jp + 1) * W]
                OP("vector", lambda e, wdj=wdj, sa=sa, sb_=sb_:
                   e.tensor_tensor(wdj.bitcast(I16), sa, sb_, ALU.bitwise_or))
                OP("vector", lambda e, wdj=wdj, W=W:
                   e.tensor_tensor(wdj, wdj, sc_t[:, 0:W], ALU.mult),
                   S["dEV"])
                pr.wait("scalar", S["dEV"])
                store(jp, wdj)
            pr.wait("sync", S["dEV"])
            pr.wait("gpsimd", S["dEV"])
            pr.wait("vector", S["dST"])

        def dec_pair(n_i32, ld_pack, ld_scale):
            ld_pack()
            ld_scale()
            pr.wait("vector", S["dLD"])
            W = 2 * n_i32
            OP("vector", lambda e, W=W: e.tensor_scalar(
                sc_t[:, 0:W], sc_raw[:, 0:W], float(2.0 ** 126), None,
                ALU.mult))
            xn16 = XN[:].bitcast(I16)
            for m in range(4):
                sa = xn16[:, 0:W]
                sb_ = xn16[:, 8192:8192 + W]
                for half, jj in ((0, 2 * m), (1, 2 * m + 1)):
                    pk16h = pk_t[half * 64:(half + 1) * 64,
                                 0:n_i32].bitcast(I16)
                    sah = sa[half * 64:(half + 1) * 64, :]
                    sbh = sb_[half * 64:(half + 1) * 64, :]
                    jp = jj % 4
                    ss, ms = _SIGN_SH[jp], _MAG_SH[jp]
                    if ss > 0:
                        OP("vector", lambda e, sah=sah, pk16h=pk16h, ss=ss:
                           e.tensor_scalar(sah, pk16h, ss, 0x8000 - 0x10000,
                                           ALU.logical_shift_left,
                                           ALU.bitwise_and))
                    else:
                        OP("vector", lambda e, sah=sah, pk16h=pk16h:
                           e.tensor_scalar(sah, pk16h, 0x8000 - 0x10000,
                                           None, ALU.bitwise_and))
                    if ms > 0:
                        OP("vector", lambda e, sbh=sbh, pk16h=pk16h, ms=ms:
                           e.tensor_scalar(sbh, pk16h, ms, 0x01C0,
                                           ALU.logical_shift_left,
                                           ALU.bitwise_and))
                    else:
                        OP("vector", lambda e, sbh=sbh, pk16h=pk16h, ms=ms:
                           e.tensor_scalar(sbh, pk16h, -ms, 0x01C0,
                                           ALU.logical_shift_right,
                                           ALU.bitwise_and))
                wdj = wd_t[:, m * W:(m + 1) * W]
                OP("vector", lambda e, wdj=wdj, sa=sa, sb_=sb_:
                   e.tensor_tensor(wdj.bitcast(I16), sa, sb_, ALU.bitwise_or))
                OP("vector", lambda e, wdj=wdj, W=W:
                   e.tensor_tensor(wdj, wdj, sc_t[:, 0:W], ALU.mult),
                   S["dEV"])
                pr.wait("scalar", S["dEV"])
                OP("scalar", lambda e, m=m, wdj=wdj: e.dma_start(
                    out=wdec_dn[(20 + m) * P:(20 + m + 1) * P, :],
                    in_=wdj), S["dST"], 16)
            pr.wait("sync", S["dEV"])
            pr.wait("gpsimd", S["dEV"])
            pr.wait("vector", S["dST"])

        GW = 2816
        for cc in range(2):
            for g in range(4):
                dec_one(
                    GW,
                    lambda cc=cc, g=g: OP("sync", lambda e: e.dma_start(
                        out=pk_t[:, 0:GW],
                        in_=gupk[cc * P:(cc + 1) * P, g * GW:(g + 1) * GW]),
                        S["dLD"], 16),
                    lambda cc=cc, g=g: OP("gpsimd", lambda e: e.dma_start(
                        out=sc_raw[:, 0:2 * GW],
                        in_=gusc[cc * P:(cc + 1) * P,
                                 g * 2 * GW:(g + 1) * 2 * GW]),
                        S["dLD"], 16),
                    lambda jp, wdj, cc=cc, g=g: OP(
                        "scalar", lambda e, jp=jp, wdj=wdj: e.dma_start(
                            out=wdec_gu[(cc * 4 + jp) * P:
                                        (cc * 4 + jp + 1) * P,
                                        g * 2 * GW:(g + 1) * 2 * GW],
                            in_=wdj), S["dST"], 16))
        DW = 2048
        for cc2 in range(6):
            if cc2 < 5:
                def ldp(cc2=cc2):
                    OP("sync", lambda e: e.dma_start(
                        out=pk_t[:, 0:DW],
                        in_=dnpk[cc2 * P:(cc2 + 1) * P, :]), S["dLD"], 16)

                def lds(cc2=cc2):
                    OP("gpsimd", lambda e: e.dma_start(
                        out=sc_raw[:, 0:2 * DW],
                        in_=dnsc[cc2 * P:(cc2 + 1) * P, :]), S["dLD"], 16)
            else:
                def ldp():
                    OP("sync", lambda e: e.dma_start(
                        out=pk_t[0:64, 0:DW], in_=dnpk[640:704, :]),
                        S["dLD"], 16)
                    OP("sync", lambda e: e.dma_start(
                        out=pk_t[64:128, 0:DW], in_=dnpk[640:704, :]),
                        S["dLD"], 16)

                def lds():
                    OP("gpsimd", lambda e: e.dma_start(
                        out=sc_raw[0:64, 0:2 * DW], in_=dnsc[640:704, :]),
                        S["dLD"], 16)
                    OP("gpsimd", lambda e: e.dma_start(
                        out=sc_raw[64:128, 0:2 * DW], in_=dnsc[640:704, :]),
                        S["dLD"], 16)
            if cc2 < 5:
                dec_one(
                    DW, ldp, lds,
                    lambda jp, wdj, cc2=cc2: OP(
                        "scalar", lambda e, jp=jp, wdj=wdj: e.dma_start(
                            out=wdec_dn[(cc2 * 4 + jp) * P:
                                        (cc2 * 4 + jp + 1) * P, :],
                            in_=wdj), S["dST"], 16))
            else:
                dec_pair(DW, ldp, lds)

    def emit_ln_tile(tt, src_dram, xn_out, ldS, peS, evS, scS):
        xa = (A1 if tt % 2 == 0 else A2)[:]
        OP("gpsimd", lambda e, xa=xa, tt=tt: e.dma_start(
            out=xa.rearrange("p (k t) -> p k t", k=NKD),
            in_=src_dram[:, tt * TT:(tt + 1) * TT]
            .rearrange("(k p) t -> p k t", p=P)), ldS, 16)
        ldv = ldS.n
        pr.wait("scalar", ldS, ldv)
        guard_bank(6)
        guard_bank(7)
        pe_base = peS.n
        pr.wait("tensor", ldS, ldv)
        for k in range(NKD):
            OP("tensor", lambda e, xa=xa, k=k: e.matmul(
                bank(6, TT)[0:1, :], TB["onesb"][:],
                xa[:, k * TT:(k + 1) * TT],
                start=(k == 0), stop=(k == NKD - 1)),
                peS if k == NKD - 1 else None)
        for k in range(NKD):
            sq = sm_sq[k % 2]
            if k >= 2:
                pr.wait("scalar", peS, pe_base + 1 + (k - 1))
            OP("scalar", lambda e, xa=xa, k=k, sq=sq: e.activation(
                sq[:, 0:TT], xa[:, k * TT:(k + 1) * TT], AF.Square), scS)
            scv = scS.n
            pr.wait("tensor", scS, scv)
            OP("tensor", lambda e, k=k, sq=sq: e.matmul(
                bank(7, TT)[0:1, :], TB["onesb"][:], sq[:, 0:TT],
                start=(k == 0), stop=(k == NKD - 1)), peS)
        pev = peS.n
        pr.wait("vector", peS, pev)
        OP("vector", lambda e: e.tensor_scalar(
            sm_mu, bank(6, TT)[0:1, :], 1.0 / D, None, ALU.mult))
        OP("vector", lambda e: e.tensor_scalar(
            sm_ms, bank(7, TT)[0:1, :], 1.0 / D, None, ALU.mult))
        OP("vector", lambda e: e.tensor_tensor(
            sm_va, sm_mu, sm_mu, ALU.mult))
        OP("vector", lambda e: e.tensor_tensor(
            sm_va, sm_ms, sm_va, ALU.subtract))
        OP("vector", lambda e: e.tensor_scalar(
            sm_va, sm_va, 1e-5, None, ALU.add), evS)
        evv = evS.n
        pr.wait("scalar", evS, evv)
        OP("scalar", lambda e: e.activation(sm_ri, sm_va, AF.Ln), scS)
        scln = scS.n
        pr.wait("vector", scS, scln)
        OP("vector", lambda e: e.tensor_scalar(
            sm_ri, sm_ri, -0.5, None, ALU.mult), evS)
        evln = evS.n
        pr.wait("scalar", evS, evln)
        OP("scalar", lambda e: e.activation(sm_ri, sm_ri, AF.Exp), scS)
        scv2 = scS.n
        pr.wait("tensor", evS, evv)
        pr.wait("tensor", scS, scv2)
        OP("tensor", lambda e: e.matmul(
            bank(6, TT), TB["onesf"][:], sm_mu, start=True, stop=True))
        OP("tensor", lambda e: e.matmul(
            bank(7, TT), TB["onesf"][:], sm_ri, start=True, stop=True), peS)
        pev2 = peS.n
        pr.wait("vector", peS, pev2)
        for k in range(NKD):
            OP("vector", lambda e, xa=xa, k=k: e.tensor_tensor(
                sm_t1[:, 0:TT], xa[:, k * TT:(k + 1) * TT], bank(6, TT),
                ALU.subtract))
            OP("vector", lambda e, k=k, tt=tt: e.tensor_tensor(
                xn_out(k, tt), sm_t1[:, 0:TT], bank(7, TT), ALU.mult),
                evS if k == NKD - 1 else None)
        set_guard(6, evS)
        set_guard(7, evS)
        pr.wait("gpsimd", evS)

    def emit_a():
        def xn_out(k, tt):
            return XN[:, k * T + tt * TT:k * T + (tt + 1) * TT]
        for tt in range(NTT):
            emit_ln_tile(tt, xT, xn_out, S["aLD"], S["aPE"], S["aEV"],
                         S["aSC"])
        if debug:
            pr.wait("gpsimd", S["aEV"])
            OP("gpsimd", lambda e: e.dma_start(
                out=xn_dbg.rearrange("(k p) t -> p k t", p=P),
                in_=XN[:].rearrange("p (k t) -> p k t", k=NKD)),
                S["fST"], 16)
            OP("gpsimd", lambda e: e.dma_start(
                out=rows_dbg, in_=ROWS[:]), S["fST"], 16)

    def proj_tile(lhsT_fn, rhs_fn, out_sl, cos_sl, sin_sl, peS, evS, bk,
                  rotbank=6):
        guard_bank(bk)
        for k in range(NKD):
            OP("tensor", lambda e, k=k: e.matmul(
                bank(bk), lhsT_fn(k), rhs_fn(k),
                start=(k == 0), stop=(k == NKD - 1)),
                peS if k == NKD - 1 else None)
        pev = peS.n
        pr.wait("vector", peS, pev)
        OP("vector", lambda e: e.tensor_copy(out_sl, bank(bk)), evS)
        set_guard(bk, evS)
        if cos_sl is None:
            return
        evv = evS.n
        guard_bank(rotbank)
        pr.wait("tensor", evS, evv)
        OP("tensor", lambda e: e.matmul(
            bank(rotbank), TB["rotm"][:], out_sl, start=True, stop=True),
            peS)
        pev2 = peS.n
        pr.wait("vector", peS, pev2)
        OP("vector", lambda e: e.tensor_tensor(
            sm_t1[:, 0:TT], out_sl, cos_sl, ALU.mult))
        OP("vector", lambda e: e.scalar_tensor_tensor(
            sm_ktmp[:, 0:TT], bank(rotbank), 1.0, sin_sl,
            ALU.mult, ALU.mult))
        OP("vector", lambda e: e.tensor_tensor(
            out_sl, sm_t1[:, 0:TT], sm_ktmp[:, 0:TT], ALU.add), evS)
        set_guard(rotbank, evS)

    def transpose_tile(src_sl, dst_sl, pos, peS, scS):
        guard_bank(7)
        OP("tensor", lambda e: e.transpose(
            bank_bf(7)[:, pos * P:(pos + 1) * P], src_sl, TB["ident"][:]),
            peS)
        pev = peS.n
        pr.wait("scalar", peS, pev)
        OP("scalar", lambda e: e.copy(
            dst_sl, bank_bf(7)[:, pos * P:(pos + 1) * P]), scS)
        set_guard(7, scS)

    def emit_p():
        cospf_sb = B1[:, 4096:6144]
        sinpf_sb = B1[:, 6144:8192]
        OP("gpsimd", lambda e: e.dma_start(out=cospf_sb, in_=cospf),
           S["pLD"], 16)
        OP("gpsimd", lambda e: e.dma_start(out=sinpf_sb, in_=sinpf),
           S["pLD"], 16)
        xnp = HB[:, 0:8192]
        kf = B1[:, 0:512]
        vf = B1[:, 512:1024]
        kt = B1[:, 1024:1536]
        vt = B1[:, 1536:2048]
        ksc = B1[:, 2048:2560]
        wk_sb = B2[:, 0:2048]
        wv_sb = B2[:, 2048:4096]

        def xnp_out(k, tt):
            return xnp[:, k * TT:(k + 1) * TT]

        for tt in range(NTT):
            emit_ln_tile(tt, xpT, xnp_out, S["pLD"], S["pPE"], S["pEV"],
                         S["aSC"])
            for g_ in range(HKV):
                pr.wait("gpsimd", S["pPE"])
                OP("gpsimd", lambda e, g_=g_: e.dma_start(
                    out=wk_sb.rearrange("p (k m) -> p k m", k=NKD),
                    in_=wkT[:, g_ * DH:(g_ + 1) * DH]
                    .rearrange("(k p) m -> p k m", p=P)), S["pLD"], 16)
                OP("gpsimd", lambda e, g_=g_: e.dma_start(
                    out=wv_sb.rearrange("p (k m) -> p k m", k=NKD),
                    in_=wvT[:, g_ * DH:(g_ + 1) * DH]
                    .rearrange("(k p) m -> p k m", p=P)), S["pLD"], 16)
                wld = S["pLD"].n
                pr.wait("tensor", S["pLD"], wld)
                proj_tile(lambda k: wk_sb[:, k * P:(k + 1) * P],
                          lambda k: xnp[:, k * TT:(k + 1) * TT],
                          kf,
                          cospf_sb[:, tt * TT:(tt + 1) * TT],
                          sinpf_sb[:, tt * TT:(tt + 1) * TT],
                          S["pPE"], S["pEV"], bk=0)
                proj_tile(lambda k: wv_sb[:, k * P:(k + 1) * P],
                          lambda k: xnp[:, k * TT:(k + 1) * TT],
                          vf, None, None, S["pPE"], S["pEV"], bk=1)
                for c4 in range(4):
                    tc = tt * 4 + c4
                    pr.wait("scalar", S["pPE"])
                    transpose_tile(kf[:, c4 * P:(c4 + 1) * P],
                                   kt[:, 0:P], 0, S["pPE"], S["bSC"])
                    transpose_tile(vf[:, c4 * P:(c4 + 1) * P],
                                   vt[:, 0:P], 1, S["pPE"], S["bSC"])
                    scv = S["bSC"].n
                    pr.wait("vector", S["bSC"], scv)
                    pr.wait("vector", S["pPE"])
                    for hi in range(4):
                        hh = 4 * g_ + hi
                        OP("vector", lambda e, hi=hi, hh=hh, tc=tc:
                           e.tensor_scalar(
                               ksc[:, hi * P:(hi + 1) * P], kt[:, 0:P],
                               TB["kdecF"][:, hh * NCHK + tc:
                                           hh * NCHK + tc + 1],
                               None, ALU.mult),
                           S["pEV"] if hi == 3 else None)
                    evv = S["pEV"].n
                    pr.wait("tensor", S["pEV"], evv)
                    for hi in range(4):
                        hh = 4 * g_ + hi
                        OP("tensor", lambda e, hi=hi, hh=hh, tc=tc: e.matmul(
                            PS[:, 1024 + hh * P:1024 + (hh + 1) * P],
                            ksc[:, hi * P:(hi + 1) * P], vt[:, 0:P],
                            start=(tc == 0), stop=(tc == NCHK - 1)),
                            S["pPE"] if hi == 3 else None)
        pr.wait("vector", S["pPE"])
        OP("vector", lambda e: e.tensor_copy(KV[:], PS[:, 1024:3072]))
        OP("vector", lambda e: e.tensor_copy(KVB[:], KV[:]), S["pEV"])
        if debug:
            pr.wait("gpsimd", S["pEV"])
            OP("gpsimd", lambda e: e.dma_start(out=kv_dbg, in_=KV[:]),
               S["fST"], 16)

    def emit_b():
        cosf_sb = HB[:, 16384:18432]
        sinf_sb = HB[:, 18432:20480]
        OP("gpsimd", lambda e: e.dma_start(out=cosf_sb, in_=cosf),
           S["bLD"], 16)
        OP("gpsimd", lambda e: e.dma_start(out=sinf_sb, in_=sinf),
           S["bLD"], 16)
        qg = B1[:]
        kf_full = B2[:, 0:2048]
        kt_full = B2[:, 2048:4096]
        og = HB[:, 0:8192]
        vf_full = HB[:, 8192:10240]
        vt_full = HB[:, 10240:12288]
        wkv = HB[:, 12288:16384]

        for g in range(NG):
            wq_sb = (A1 if g % 2 == 0 else A2)[:]
            if g >= 1:
                pr.wait("gpsimd", S["bST"])
                pr.wait("gpsimd", S["bEV"])
            OP("gpsimd", lambda e, g=g, wq_sb=wq_sb: e.dma_start(
                out=wq_sb.rearrange("p (k m) -> p k m", k=NKD),
                in_=wqT[:, g * 512:(g + 1) * 512]
                .rearrange("(k p) m -> p k m", p=P)), S["bLD"], 16)
            wqv = S["bLD"].n
            OP("gpsimd", lambda e, g=g: e.dma_start(
                out=wkv[:, 0:2048].rearrange("p (k m) -> p k m", k=NKD),
                in_=wkT[:, g * DH:(g + 1) * DH]
                .rearrange("(k p) m -> p k m", p=P)), S["bLD"], 16)
            OP("gpsimd", lambda e, g=g: e.dma_start(
                out=wkv[:, 2048:4096].rearrange("p (k m) -> p k m", k=NKD),
                in_=wvT[:, g * DH:(g + 1) * DH]
                .rearrange("(k p) m -> p k m", p=P)), S["bLD"], 16)
            wkvv = S["bLD"].n

            pr.wait("tensor", S["bLD"], wqv)
            for ec in range(4):
                for t4 in range(NTT):
                    proj_tile(
                        lambda k, ec=ec, wq_sb=wq_sb: wq_sb[
                            :, k * 512 + ec * P:k * 512 + (ec + 1) * P],
                        lambda k, t4=t4: XN[:, k * T + t4 * TT:
                                            k * T + (t4 + 1) * TT],
                        qg[:, ec * T + t4 * TT:ec * T + (t4 + 1) * TT],
                        cosf_sb[:, t4 * TT:(t4 + 1) * TT],
                        sinf_sb[:, t4 * TT:(t4 + 1) * TT],
                        S["bPE"], S["bEV"], bk=(ec * NTT + t4) % 2)
                    if debug and g == 0 and ec == 0 and t4 == 0:
                        pr.wait("vector", S["bEV"])
                        OP("vector", lambda e: e.tensor_copy(
                            sm_ev[0][:, 0:TT], bank(0)), S["bEV"])
                        pr.wait("gpsimd", S["bEV"])
                        OP("gpsimd", lambda e: e.dma_start(
                            out=kv2_dbg[:, 0:512], in_=sm_ev[0][:, 0:TT]),
                            S["fST"], 16)
                        OP("gpsimd", lambda e: e.dma_start(
                            out=kv2_dbg[:, 512:1024], in_=qg[:, 0:TT]),
                            S["fST"], 16)
                        OP("gpsimd", lambda e: e.dma_start(
                            out=kv2_dbg[:, 1024:1536], in_=XN[:, 0:TT]),
                            S["fST"], 16)
                        OP("gpsimd", lambda e, wq_sb=wq_sb: e.dma_start(
                            out=kv2_dbg[:, 1536:2048], in_=wq_sb[:, 0:TT]),
                            S["fST"], 16)
            pr.wait("tensor", S["bLD"], wkvv)
            for t4 in range(NTT):
                proj_tile(
                    lambda k: wkv[:, k * P:(k + 1) * P],
                    lambda k, t4=t4: XN[:, k * T + t4 * TT:
                                        k * T + (t4 + 1) * TT],
                    kf_full[:, t4 * TT:(t4 + 1) * TT],
                    cosf_sb[:, t4 * TT:(t4 + 1) * TT],
                    sinf_sb[:, t4 * TT:(t4 + 1) * TT],
                    S["bPE"], S["bEV"], bk=t4 % 2)
                proj_tile(
                    lambda k: wkv[:, 2048 + k * P:2048 + (k + 1) * P],
                    lambda k, t4=t4: XN[:, k * T + t4 * TT:
                                        k * T + (t4 + 1) * TT],
                    vf_full[:, t4 * TT:(t4 + 1) * TT],
                    None, None, S["bPE"], S["bEV"], bk=t4 % 2)
            for c in range(NCHK):
                transpose_tile(kf_full[:, c * P:(c + 1) * P],
                               kt_full[:, c * P:(c + 1) * P],
                               0, S["bPE"], S["bSC"])
                transpose_tile(vf_full[:, c * P:(c + 1) * P],
                               vt_full[:, c * P:(c + 1) * P],
                               1, S["bPE"], S["bSC"])
            trv = S["bSC"].n
            pr.wait("vector", S["bSC"], trv)
            pr.wait("tensor", S["bSC"], trv)
            if False and debug and g == 0:
                pr.wait("gpsimd", S["bEV"])
                pr.wait("gpsimd", S["bSC"])
                OP("gpsimd", lambda e: e.dma_start(out=qg_dbg, in_=qg),
                   S["fST"], 16)
                OP("gpsimd", lambda e: e.dma_start(
                    out=kv2_dbg[:, 0:2048], in_=kf_full), S["fST"], 16)
                OP("gpsimd", lambda e: e.dma_start(
                    out=kv2_dbg[:, 2048:4096], in_=vt_full), S["fST"], 16)
                OP("gpsimd", lambda e: e.dma_start(
                    out=kv2_dbg[:, 4096:6144], in_=KVB[:]), S["fST"], 16)
                OP("gpsimd", lambda e, wq_sb=wq_sb: e.dma_start(
                    out=wq_dbg, in_=wq_sb), S["fST"], 16)

            pe_mm = {}
            mask_ev = {}
            oev_sc = {}
            st_ev = {-1: S["bEV"].n}
            dsl = TB["diagT"][:, g * 512:(g + 1) * 512]
            qdsl = TB["qdecT"][:, g * 512:(g + 1) * 512]
            for c in range(NCHK):
                par = c % 2
                if c >= 2:
                    pr.wait("vector", S["bPE"], pe_mm[c - 2])
                qg3 = qg.rearrange("p (hh t) -> p hh t", hh=4)[
                    :, :, c * P:(c + 1) * P]
                qd3 = qdsl.rearrange("p (hh t) -> p hh t", hh=4)
                qp3 = sm_qp[par].rearrange("p (hh t) -> p hh t", hh=4)
                OP("vector", lambda e, qg3=qg3, qd3=qd3, qp3=qp3:
                   e.tensor_tensor(qp3, qg3, qd3, ALU.mult))
                for hi in range(4):
                    hh = 4 * g + hi
                    OP("vector", lambda e, hi=hi, hh=hh, c=c, par=par:
                       e.tensor_scalar(
                           sm_kd[par][:, hi * P:(hi + 1) * P],
                           kt_full[:, c * P:(c + 1) * P],
                           TB["kdec"][:, hh:hh + 1], None, ALU.mult),
                       S["bEV"] if hi == 3 else None)
                prep_ev = S["bEV"].n
                if c >= 2:
                    pr.wait("tensor", S["bEV"], mask_ev[c - 2])
                for hi in range(4):
                    OP("tensor", lambda e, hi=hi, c=c, par=par: e.matmul(
                        bank(par)[:, hi * P:(hi + 1) * P],
                        kf_full[:, c * P:(c + 1) * P],
                        qg[:, hi * T + c * P:hi * T + (c + 1) * P],
                        start=True, stop=True),
                        S["bPE"] if hi == 3 else None)
                pe_sc = S["bPE"].n
                pr.wait("vector", S["bPE"], pe_sc)
                OP("vector", lambda e, par=par, dsl=dsl: e.tensor_tensor(
                    sm_sT[par][:, 0:512], bank(par), dsl, ALU.mult),
                    S["bEV"])
                mask_ev[c] = S["bEV"].n
                pr.wait("tensor", S["bEV"], mask_ev[c])
                pr.wait("tensor", S["bEV"], prep_ev)
                pr.wait("tensor", S["bEV"], st_ev[c - 1])
                if c >= 2:
                    pr.wait("tensor", S["bSC"], oev_sc[c - 2])
                for hi in range(4):
                    hh = 4 * g + hi
                    OP("tensor", lambda e, hi=hi, c=c, par=par: e.matmul(
                        bank(2 + par)[:, hi * P:(hi + 1) * P],
                        vt_full[:, c * P:(c + 1) * P],
                        sm_sT[par][:, hi * P:(hi + 1) * P],
                        start=True, stop=False))
                    OP("tensor", lambda e, hi=hi, hh=hh, par=par: e.matmul(
                        bank(2 + par)[:, hi * P:(hi + 1) * P],
                        KVB[:, hh * P:(hh + 1) * P],
                        sm_qp[par][:, hi * P:(hi + 1) * P],
                        start=False, stop=True))
                for hi in range(4):
                    OP("tensor", lambda e, hi=hi, c=c, par=par: e.matmul(
                        bank(4 + par)[:, hi * P:(hi + 1) * P],
                        sm_kd[par][:, hi * P:(hi + 1) * P],
                        vt_full[:, c * P:(c + 1) * P],
                        start=True, stop=True),
                        S["bPE"] if hi == 3 else None)
                pe_mm[c] = S["bPE"].n
                pr.wait("scalar", S["bPE"], pe_mm[c])
                for hi in range(4):
                    OP("scalar", lambda e, hi=hi, c=c, par=par: e.copy(
                        og[:, hi * T + c * P:hi * T + (c + 1) * P],
                        bank(2 + par)[:, hi * P:(hi + 1) * P]),
                        S["bSC"] if hi == 3 else None)
                oev_sc[c] = S["bSC"].n
                pr.wait("vector", S["bPE"], pe_mm[c])
                for hi in range(4):
                    hh = 4 * g + hi
                    OP("vector", lambda e, hi=hi, hh=hh, par=par:
                       e.scalar_tensor_tensor(
                           KV[:, hh * P:(hh + 1) * P],
                           KV[:, hh * P:(hh + 1) * P], BLK[hh],
                           bank(4 + par)[:, hi * P:(hi + 1) * P],
                           ALU.mult, ALU.add))
                OP("vector", lambda e, g=g: e.tensor_copy(
                    KVB[:, g * 512:(g + 1) * 512],
                    KV[:, g * 512:(g + 1) * 512]), S["bEV"])
                st_ev[c] = S["bEV"].n
            pr.wait("gpsimd", S["bSC"], oev_sc[NCHK - 1])
            OP("gpsimd", lambda e, g=g: e.dma_start(
                out=o_bnc[g * 512:(g + 1) * 512, :]
                .rearrange("(hh p) t -> p hh t", p=P),
                in_=og.rearrange("p (hh t) -> p hh t", hh=4)), S["bST"], 16)
            set_guard(0, S["bEV"])
            set_guard(1, S["bEV"])
            set_guard(2, S["bSC"])
            set_guard(3, S["bSC"])
            set_guard(4, S["bEV"])
            set_guard(5, S["bEV"])

    def emit_op():
        pr.wait("sync", S["bST"])
        pr.wait("tensor", S["bEV"])
        pr.wait("tensor", S["bSC"])
        pr.wait("vector", S["bST"])
        pr.wait("scalar", S["bST"])
        pr.wait("gpsimd", S["bST"])
        for t4 in range(NTT):
            ot = B1[:]
            if t4 >= 1:
                pr.wait("sync", S["oPE"])
            OP("sync", lambda e, t4=t4: e.dma_start(
                out=ot.rearrange("p (k t) -> p k t", k=NKD),
                in_=o_bnc[:, t4 * TT:(t4 + 1) * TT]
                .rearrange("(k p) t -> p k t", p=P)), S["oLD"], 16)
            otv = S["oLD"].n
            for dc in list(range(NKD))[::-1]:
                slot = dc % 4
                wo_sb = (A1 if slot < 2 else A2)[
                    :, (slot % 2) * 4096:(slot % 2) * 4096 + 2048]
                if t4 * NKD + dc >= 4:
                    pr.wait("gpsimd", S["oPE"])
                OP("gpsimd", lambda e, dc=dc, wo_sb=wo_sb: e.dma_start(
                    out=wo_sb.rearrange("p (k m) -> p k m", k=NKD),
                    in_=woT[:, dc * P:(dc + 1) * P]
                    .rearrange("(k p) m -> p k m", p=P)), S["oLD"], 16)
                wov = S["oLD"].n
                bk = dc % 2
                guard_bank(bk)
                pr.wait("tensor", S["oLD"], otv)
                pr.wait("tensor", S["oLD"], wov)
                for k in range(NKD):
                    OP("tensor", lambda e, k=k, wo_sb=wo_sb, bk=bk: e.matmul(
                        bank(bk), wo_sb[:, k * P:(k + 1) * P],
                        ot[:, k * TT:(k + 1) * TT],
                        start=(k == 0), stop=(k == NKD - 1)),
                        S["oPE"] if k == NKD - 1 else None)
                pev = S["oPE"].n
                ev = sm_ev[0] if dc % 2 == 0 else sm_sq[0]
                pr.wait("vector", S["oPE"], pev)
                if t4 * NKD + dc >= 2:
                    pr.wait("vector", S["oST"])
                OP("vector", lambda e, ev=ev, bk=bk: e.tensor_copy(
                    ev[:, 0:TT], bank(bk)), S["oEV"])
                set_guard(bk, S["oEV"])
                evv = S["oEV"].n
                pr.wait("scalar", S["oEV"], evv)
                OP("scalar", lambda e, ev=ev, dc=dc, t4=t4: e.dma_start(
                    out=at_bnc[dc * P:(dc + 1) * P, t4 * TT:(t4 + 1) * TT],
                    in_=ev[:, 0:TT]), S["oST"], 16)

    def emit_c(sub=3):
        pr.wait("sync", S["oST"])
        h_sb = HB[:]
        yt = B1[:]
        for t4 in range(NTT):
            for up in ((0,) if sub == 1 else (0, 1)):
                for cg in range(11):
                    wg = (A1 if cg % 2 == 0 else A2)[:]
                    if t4 + up + cg > 0:
                        pr.wait("sync", S["cPE"])
                    base = 11264 * up + 1024 * cg
                    OP("sync", lambda e, base=base, wg=wg: e.dma_start(
                        out=wg.rearrange("p (b m) -> p b m", b=8),
                        in_=wdec_gu.rearrange("(b p) m -> p b m", p=P)[
                            :, :, base:base + 1024]), S["cLD"], 16)
                    wgv = S["cLD"].n
                    pr.wait("tensor", S["cLD"], wgv)
                    for ti in range(4):
                        t = 4 * cg + ti
                        bk = t % 2
                        guard_bank(bk)
                        for k in range(NKD):
                            b, par = _gu_block(k)
                            OP("tensor",
                               lambda e, k=k, b=b, par=par, ti=ti, t4=t4,
                               bk=bk, wg=wg: e.matmul(
                                   bank(bk),
                                   wg[:, b * 1024 + ti * 256 + par:
                                      b * 1024 + (ti + 1) * 256:2],
                                   XN[:, k * T + t4 * TT:
                                      k * T + (t4 + 1) * TT],
                                   start=(k == 0), stop=(k == NKD - 1)),
                               S["cPE"] if k == NKD - 1 else None)
                        pev = S["cPE"].n
                        if up == 0:
                            pr.wait("scalar", S["cPE"], pev)
                            OP("scalar", lambda e, t=t, bk=bk: e.activation(
                                h_sb[:, t * TT:(t + 1) * TT], bank(bk),
                                AF.Silu), S["cSC"])
                            set_guard(bk, S["cSC"])
                        else:
                            pr.wait("vector", S["cPE"], pev)
                            OP("vector", lambda e, t=t, bk=bk:
                               e.tensor_tensor(
                                   h_sb[:, t * TT:(t + 1) * TT],
                                   h_sb[:, t * TT:(t + 1) * TT],
                                   bank(bk), ALU.mult), S["cEV"])
                            set_guard(bk, S["cEV"])
            if sub < 3:
                continue
            pr.wait("tensor", S["cEV"])
            for dc in range(NKD):
                wdn = (A1 if dc % 2 == 0 else A2)[:, 0:6144]
                pr.wait("sync", S["cPE"])
                OP("sync", lambda e, dc=dc, wdn=wdn: e.dma_start(
                    out=wdn.rearrange("p (b m) -> p b m", b=24),
                    in_=wdec_dn.rearrange("(b p) m -> p b m", p=P)[
                        :, :, dc * 256:(dc + 1) * 256]), S["cLD"], 16)
                wdv = S["cLD"].n
                bk = dc % 2
                guard_bank(bk)
                pr.wait("tensor", S["cLD"], wdv)
                for t in range(NIC):
                    if t < 40:
                        b, par = _dn_block(t)
                    else:
                        b, par = 20 + (t - 40), (t - 40) // 2
                    OP("tensor", lambda e, t=t, b=b, par=par, wdn=wdn, bk=bk:
                       e.matmul(
                           bank(bk),
                           wdn[:, b * 256 + par:(b + 1) * 256:2],
                           h_sb[:, t * TT:(t + 1) * TT],
                           start=(t == 0), stop=(t == NIC - 1)),
                       S["cPE"] if t == NIC - 1 else None)
                pev = S["cPE"].n
                pr.wait("vector", S["cPE"], pev)
                if t4 > 0 and dc == 0:
                    pr.wait("vector", S["cST"])
                OP("vector", lambda e, dc=dc, bk=bk: e.tensor_copy(
                    yt[:, dc * TT:(dc + 1) * TT], bank(bk)), S["cEV"])
                set_guard(bk, S["cEV"])
            evall = S["cEV"].n
            pr.wait("scalar", S["cEV"], evall)
            guard_bank(6)
            sc_base = S["cSC"].n
            pe_base6 = S["cPE"].n
            for dc in range(NKD):
                sq = sm_sq[dc % 2]
                if dc >= 2:
                    pr.wait("scalar", S["cPE"], pe_base6 + dc - 1)
                OP("scalar", lambda e, dc=dc, sq=sq: e.activation(
                    sq[:, 0:TT], yt[:, dc * TT:(dc + 1) * TT],
                    AF.Square), S["cSC"])
                scv = S["cSC"].n
                pr.wait("tensor", S["cSC"], scv)
                OP("tensor", lambda e, dc=dc, sq=sq: e.matmul(
                    bank(6, TT)[0:1, :], TB["onesb"][:], sq[:, 0:TT],
                    start=(dc == 0), stop=(dc == NKD - 1)), S["cPE"])
            pev = S["cPE"].n
            pr.wait("vector", S["cPE"], pev)
            OP("vector", lambda e, t4=t4: e.tensor_copy(
                MSQ[:, t4 * TT:(t4 + 1) * TT], bank(6, TT)[0:1, :]),
                S["cEV"])
            set_guard(6, S["cEV"])
            pr.wait("gpsimd", S["cEV"])
            OP("gpsimd", lambda e, t4=t4: e.dma_start(
                out=y_bnc[:, t4 * TT:(t4 + 1) * TT]
                .rearrange("(k p) t -> p k t", p=P),
                in_=yt.rearrange("p (k t) -> p k t", k=NKD)), S["cST"], 16)

    def emit_d():
        xf = XN[:].bitcast(F32)
        att = B1[:]
        yb = HB[:, 0:8192]
        for t4 in range(NTT):
            half_ld = xf[:, (t4 % 2) * 8192:(t4 % 2) * 8192 + 8192]
            if t4 >= 2:
                pr.wait("sync", S["fEV"])
            if t4 >= 1:
                pr.wait("sync", S["fST"])
            OP("sync", lambda e, half_ld=half_ld, t4=t4: e.dma_start(
                out=half_ld.rearrange("p (k t) -> p k t", k=NKD),
                in_=xT[:, t4 * TT:(t4 + 1) * TT]
                .rearrange("(k p) t -> p k t", p=P)), S["fLD"], 16)
            OP("sync", lambda e, t4=t4: e.dma_start(
                out=att.rearrange("p (k t) -> p k t", k=NKD),
                in_=at_bnc[:, t4 * TT:(t4 + 1) * TT]
                .rearrange("(k p) t -> p k t", p=P)), S["fLD"], 16)
            OP("sync", lambda e, t4=t4: e.dma_start(
                out=yb.rearrange("p (k t) -> p k t", k=NKD),
                in_=y_bnc[:, t4 * TT:(t4 + 1) * TT]
                .rearrange("(k p) t -> p k t", p=P)), S["fLD"], 16)
            ldv = S["fLD"].n
            OP("vector", lambda e, t4=t4: e.tensor_scalar(
                d_va, MSQ[:, t4 * TT:(t4 + 1) * TT], 1.0 / D, 1e-6,
                ALU.mult, ALU.add), S["fEV"])
            pr.wait("scalar", S["fEV"])
            OP("scalar", lambda e: e.activation(d_va, d_va, AF.Ln),
               S["fSC"])
            pr.wait("vector", S["fSC"])
            OP("vector", lambda e: e.tensor_scalar(
                d_ri, d_va, -0.5, None, ALU.mult), S["fEV"])
            pr.wait("scalar", S["fEV"])
            OP("scalar", lambda e: e.activation(d_ri, d_ri, AF.Exp),
               S["fSC"])
            scv = S["fSC"].n
            guard_bank(7)
            pr.wait("tensor", S["fSC"], scv)
            OP("tensor", lambda e: e.matmul(
                bank(7, TT), TB["onesf"][:], d_ri, start=True, stop=True),
                S["fPE"])
            pev = S["fPE"].n
            pr.wait("vector", S["fPE"], pev)
            pr.wait("vector", S["fLD"], ldv)
            for k in range(NKD):
                outh = (A1 if k < 8 else A2)[:].bitcast(F32)
                kk = k % 8
                OP("vector", lambda e, k=k, kk=kk, outh=outh,
                   half_ld=half_ld: e.tensor_tensor(
                       outh[:, kk * TT:(kk + 1) * TT],
                       half_ld[:, k * TT:(k + 1) * TT],
                       att[:, k * TT:(k + 1) * TT], ALU.add))
                OP("vector", lambda e, k=k: e.tensor_tensor(
                    sm_t1[:, 0:TT], yb[:, k * TT:(k + 1) * TT],
                    bank(7, TT), ALU.mult))
                OP("vector", lambda e, kk=kk, outh=outh: e.tensor_tensor(
                    outh[:, kk * TT:(kk + 1) * TT],
                    outh[:, kk * TT:(kk + 1) * TT],
                    sm_t1[:, 0:TT], ALU.add),
                    S["fEV"] if (k == 7 or k == NKD - 1) else None)
            set_guard(7, S["fEV"])
            ev1 = S["fEV"].n - 1
            ev2 = S["fEV"].n
            pr.wait("gpsimd", S["fEV"], ev1)
            OP("gpsimd", lambda e, t4=t4: e.dma_start(
                out=outT.rearrange("(k p) t -> p k t", p=P)[
                    :, 0:8, t4 * TT:(t4 + 1) * TT],
                in_=A1[:].bitcast(F32).rearrange("p (k t) -> p k t", k=8)),
                S["fST"], 16)
            pr.wait("gpsimd", S["fEV"], ev2)
            OP("gpsimd", lambda e, t4=t4: e.dma_start(
                out=outT.rearrange("(k p) t -> p k t", p=P)[
                    :, 8:16, t4 * TT:(t4 + 1) * TT],
                in_=A2[:].bitcast(F32).rearrange("p (k t) -> p k t", k=8)),
                S["fST"], 16)
            pr.wait("vector", S["fST"])

        pr.wait("gpsimd", S["fST"])

    emit_tables()
    if "dec" in phases:
        emit_decode()
    if "a" in phases:
        pr.wait("gpsimd", S["dEV"])
        pr.wait("gpsimd", S["dST"])
        pr.wait("vector", S["dST"])
        emit_a()
    if "p" in phases:
        emit_p()
    if "b" in phases:
        pr.wait("tensor", S["pEV"])
        emit_b()
    if "op" in phases:
        emit_op()
    if "c" in phases:
        emit_c()
    elif "c1" in phases:
        emit_c(sub=1)
    elif "c2" in phases:
        emit_c(sub=2)
    elif "c4" in phases:
        emit_c(sub=4)
    elif "c5" in phases:
        emit_c(sub=5)
    elif "c6" in phases:
        emit_c(sub=6)
    if "d" in phases:
        pr.wait("sync", S["cST"])
        pr.wait("sync", S["cEV"])
        emit_d()
    else:
        pr.wait("gpsimd", S["aEV"])
        OP("gpsimd", lambda e: e.dma_start(
            out=outT.rearrange("(k p) t -> p k t", p=P),
            in_=XN[:].rearrange("p (k t) -> p k t", k=NKD).bitcast(BF16))
            if False else e.dma_start(
            out=outT[0:P, :], in_=SM[:, 0:T]), S["fST"], 16)
        pr.wait("gpsimd", S["fST"])

    with nc.Block() as block:
        @block.sync
        def _(sync):
            pr.run(sync, "sync")

        @block.scalar
        def _(scalar):
            pr.run(scalar, "scalar")

        @block.vector
        def _(vector):
            pr.run(vector, "vector")

        @block.tensor
        def _(tensor):
            pr.run(tensor, "tensor")

        @block.gpsimd
        def _(gpsimd):
            pr.run(gpsimd, "gpsimd")

    ctx.close()
    return nc


def _tables_np():
    pd = _perm_d()
    pi = _perm_i()
    inv = 1.0 / 10000.0 ** (np.arange(0, DH, 2) / DH)
    th_all = np.arange(4096)[:, None] * inv[None, :]
    cos_all = np.cos(th_all)
    sin_all = np.sin(th_all)

    def featmaj(tbl, tok0):
        s = tbl[tok0:tok0 + T].T
        return np.concatenate([s, s], axis=0).astype(ml_dtypes.bfloat16)

    r = np.arange(CH)
    diagT = np.zeros((P, H * CH), np.float32)
    qdecT = np.zeros((P, H * CH), np.float32)
    kdec = np.zeros((P, H), np.float32)
    kdecF = np.zeros((P, H * NCHK), np.float32)
    for hh in range(H):
        dm = np.where(r[None, :] >= r[:, None],
                      np.exp(-LAM[hh] * (r[None, :] - r[:, None])), 0.0)
        diagT[:, hh * CH:(hh + 1) * CH] = dm
        qdecT[:, hh * CH:(hh + 1) * CH] = np.exp(
            -LAM[hh] * (r + 1.0))[None, :]
        kdec[:, hh] = np.exp(-LAM[hh] * (CH - 1.0 - r))
        for tc in range(NCHK):
            kdecF[:, hh * NCHK + tc] = np.exp(
                -LAM[hh] * (2047.0 - (CH * tc + r)))
    rotm = np.zeros((P, P), np.float32)
    for m in range(64):
        rotm[m + 64, m] = -1.0
        rotm[m, m + 64] = 1.0
    return dict(pd=pd, pi=pi, cos_all=cos_all, sin_all=sin_all,
                featmaj=featmaj, diagT=diagT.astype(ml_dtypes.bfloat16),
                qdecT=qdecT.astype(ml_dtypes.bfloat16), kdec=kdec,
                kdecF=kdecF, rotm=rotm.astype(ml_dtypes.bfloat16))


def stage_inputs(inputs):
    t = _tables_np()
    pd, pi = t["pd"], t["pi"]
    x = np.asarray(inputs["L0_attn_x"], np.float32)
    wq = np.asarray(inputs["L0_attn_wq"], np.float32)
    wk = np.asarray(inputs["L0_attn_wk"], np.float32)
    wv = np.asarray(inputs["L0_attn_wv"], np.float32)
    wo = np.asarray(inputs["L0_attn_wo"], np.float32)
    gup = np.asarray(inputs["L0_ffn_gate_up_packed"], np.int32)
    gus = np.asarray(inputs["L0_ffn_gate_up_scales"])
    dnp_ = np.asarray(inputs["L0_ffn_down_packed"], np.int32)
    dns = np.asarray(inputs["L0_ffn_down_scales"])

    row_perm = np.concatenate([pi, IDIM + pi])
    shared = {
        "wqT": np.ascontiguousarray(wq.T[pd]),
        "wkT": np.ascontiguousarray(wk.T[pd]),
        "wvT": np.ascontiguousarray(wv.T[pd]),
        "woT": np.ascontiguousarray(wo[pd].T),
        "gupk": np.ascontiguousarray(gup[row_perm].T),
        "gusc": np.ascontiguousarray(
            np.repeat(np.repeat(gus[:, row_perm], 16, axis=0), 2, axis=1)
            .astype(np.float16)),
        "dnpk": np.ascontiguousarray(dnp_[pd].T),
        "dnsc": np.ascontiguousarray(
            np.repeat(np.repeat(dns[:, pd], 16, axis=0), 2, axis=1)
            .astype(np.float16)),
        "diagT": t["diagT"], "qdecT": t["qdecT"],
        "kdec": t["kdec"], "kdecF": t["kdecF"],
        "rotm": t["rotm"],
        "ident": np.eye(P, dtype=ml_dtypes.bfloat16),
        "onesb": np.ones((P, 1), ml_dtypes.bfloat16),
        "onesf": np.ones((1, P), np.float32),
        "consts": np.broadcast_to(
            np.array([1e-5, 1e-6, -0.5, 1.0 / D], np.float32), (P, 4)
        ).copy(),
    }
    in_maps = []
    for c in range(8):
        b, hf = c // 2, c % 2
        tok0 = hf * T
        xc = np.ascontiguousarray(x[b, tok0:tok0 + T][:, pd].T)
        if hf == 1:
            xp = np.ascontiguousarray(x[b, 0:T][:, pd].T)
        else:
            xp = np.zeros((D, T), np.float32)
        m = dict(shared)
        m["xT"] = xc
        m["xpT"] = xp
        m["cosf"] = t["featmaj"](t["cos_all"], tok0)
        m["sinf"] = t["featmaj"](t["sin_all"], tok0)
        m["cospf"] = t["featmaj"](t["cos_all"], 0)
        m["sinpf"] = t["featmaj"](t["sin_all"], 0)
        in_maps.append(m)
    return in_maps, pd


def kernel(**inputs):
    in_maps, pd = stage_inputs(inputs)
    nc = build_nc(debug=False)
    res = run_bass_kernel_spmd(nc, in_maps, core_ids=list(range(8)))
    x = np.asarray(inputs["L0_attn_x"])
    out = np.empty((4, 4096, 2048), np.float32)
    for c in range(8):
        b, hf = c // 2, c % 2
        blk = out[b, hf * T:(hf + 1) * T]
        blk[:, pd] = res.results[c]["outT"].T
    return out


# version 06
# speedup vs baseline: 1.4050x; 1.4050x over previous
import numpy as np
import ml_dtypes
import concourse.bass as bass
import concourse.mybir as mybir
from concourse.bass_utils import run_bass_kernel_spmd
from contextlib import ExitStack

F32 = mybir.dt.float32
BF16 = mybir.dt.bfloat16
F16 = mybir.dt.float16
I32 = mybir.dt.int32
I16 = mybir.dt.int16
ALU = mybir.AluOpType
AF = mybir.ActivationFunctionType

P = 128
H, HKV, DH, D = 16, 4, 128, 2048
T = 2048
CH = 128
NCHK = 16
NKD = 16
IDIM = 5632
NIC = 44
TT = 512
NTT = 4
NG = 4

LAM = np.exp2(-8.0 * np.arange(1, H + 1) / H)
BLK = [float(np.exp(-LAM[hh] * CH)) for hh in range(H)]
_SIGN_SH = [12, 8, 4, 0]
_MAG_SH = [6, 2, -2, -6]
ENGS = ("sync", "scalar", "vector", "tensor", "gpsimd")


def _perm_d():
    p = np.zeros(D, dtype=np.int64)
    for k in range(NKD):
        j, cc = k // 2, k % 2
        pp = np.arange(P)
        p[k * P:(k + 1) * P] = 8 * (cc * P + pp) + j
    return p


def _perm_i():
    p = np.zeros(IDIM, dtype=np.int64)
    for t in range(40):
        j2, cc2 = t // 5, t % 5
        pp = np.arange(P)
        p[t * P:(t + 1) * P] = 8 * (cc2 * P + pp) + j2
    for m in range(4):
        t = 40 + m
        for pp in range(P):
            if pp < 64:
                p[t * P + pp] = 8 * (640 + pp) + 2 * m
            else:
                p[t * P + pp] = 8 * (640 + pp - 64) + 2 * m + 1
    return p


def _gu_block(k):
    j, cc = k // 2, k % 2
    return cc * 4 + (j % 4), j // 4


def _dn_block(t):
    j2, cc2 = t // 5, t % 5
    return cc2 * 4 + (j2 % 4), j2 // 4


def _dn_pair_block(j2):
    return 20 + (j2 % 4), j2 // 4


class Sem:
    def __init__(self, h):
        self.h = h
        self.n = 0


class Prog:
    def __init__(self):
        self.streams = {e: [] for e in ENGS}
        self._lw = {}

    def em(self, eng, fn, desc=""):
        self.streams[eng].append(fn)
        if not hasattr(self, "log"):
            self.log = []
        self.log.append((eng, desc or getattr(fn, "_desc", "")))

    def wait(self, eng, sem, val=None):
        v = sem.n if val is None else val
        key = (eng, id(sem))
        if v <= 0 or self._lw.get(key, -1) >= v:
            return
        self._lw[key] = v
        h = sem.h
        self.em(eng, lambda e: e.wait_ge(h, v))

    def run(self, eng_obj, name):
        for fn in self.streams[name]:
            fn(eng_obj)


def build_nc(debug=False, expose=False,
             phases=("dec", "a", "p", "b", "op", "c", "d")):
    nc = bass.Bass("TRN2", target_bir_lowering=False)

    def din(name, shape, dt):
        return nc.dram_tensor(name, shape, dt, kind="ExternalInput").ap()

    xT = din("xT", [D, T], F32)
    xpT = din("xpT", [D, T], F32)
    wqT = din("wqT", [D, H * DH], F32)
    wkT = din("wkT", [D, HKV * DH], F32)
    wvT = din("wvT", [D, HKV * DH], F32)
    woT = din("woT", [H * DH, D], F32)
    gupk = din("gupk", [256, 11264], I32)
    gusc = din("gusc", [256, 22528], F16)
    dnpk = din("dnpk", [704, 2048], I32)
    dnsc = din("dnsc", [704, 4096], F16)
    cosf = din("cosf", [P, T], BF16)
    sinf = din("sinf", [P, T], BF16)
    cospf = din("cospf", [P, T], BF16)
    sinpf = din("sinpf", [P, T], BF16)
    diagT = din("diagT", [P, H * CH], BF16)
    qdecT = din("qdecT", [P, H * CH], BF16)
    kdec = din("kdec", [P, H], F32)
    kdecF = din("kdecF", [P, H * NCHK], F32)
    rotm = din("rotm", [P, P], BF16)
    ident = din("ident", [P, P], BF16)
    onesb = din("onesb", [P, 1], BF16)
    onesf = din("onesf", [1, P], F32)
    consts = din("consts", [P, 4], F32)

    okind = "ExternalOutput" if (debug or expose) else "Internal"
    outT = nc.dram_tensor("outT", [D, T], F32, kind="ExternalOutput").ap()
    wdec_gu = nc.dram_tensor("wdec_gu", [8 * P, 22528], BF16,
                             kind="Internal").ap()
    wqb = nc.dram_tensor("wqb", [D, H * DH], BF16, kind="Internal").ap()
    wkb = nc.dram_tensor("wkb", [D, HKV * DH], BF16, kind="Internal").ap()
    wvb = nc.dram_tensor("wvb", [D, HKV * DH], BF16, kind="Internal").ap()
    wob = nc.dram_tensor("wob", [H * DH, D], BF16, kind="Internal").ap()
    wdec_dn = nc.dram_tensor("wdec_dn", [24 * P, 4096], BF16,
                             kind="Internal").ap()
    o_bnc = nc.dram_tensor("o_bnc", [H * DH, T], BF16, kind=okind).ap()
    at_bnc = nc.dram_tensor("at_bnc", [D, T], BF16, kind=okind).ap()
    y_bnc = nc.dram_tensor("y_bnc", [D, T], BF16, kind=okind).ap()
    xn_dbg = (nc.dram_tensor("xn_dbg", [D, T], BF16,
                             kind="ExternalOutput").ap() if debug else None)
    kv_dbg = (nc.dram_tensor("kv_dbg", [P, H * DH], F32,
                             kind="ExternalOutput").ap() if debug else None)
    rows_dbg = (nc.dram_tensor("rows_dbg", [1, 3072], F32,
                               kind="ExternalOutput").ap() if debug else None)
    qg_dbg = (nc.dram_tensor("qg_dbg", [P, 8192], BF16,
                             kind="ExternalOutput").ap() if debug else None)
    wq_dbg = (nc.dram_tensor("wq_dbg", [P, 8192], BF16,
                             kind="ExternalOutput").ap() if debug else None)
    kv2_dbg = (nc.dram_tensor("kv2_dbg", [P, 4096 + 2048], BF16,
                              kind="ExternalOutput").ap() if debug else None)

    ctx = ExitStack()
    sbt = nc.alloc_sbuf_tensor
    XN = sbt("XN", [P, NKD * T], BF16)
    A1 = sbt("A1", [P, 8192], BF16)
    A2 = sbt("A2", [P, 8192], BF16)
    HB = sbt("HB", [P, NIC * TT], BF16)
    B1 = sbt("B1", [P, 8192], BF16)
    B2 = sbt("B2", [P, 4096], BF16)
    KV = sbt("KV", [P, H * DH], F32)
    KVB = sbt("KVB", [P, H * DH], BF16)
    SM = sbt("SM", [P, 2304], F32)
    ROWS = sbt("ROWS", [1, 3072], F32)
    MSQ = ROWS[0:1, 0:T]
    TB = {}
    for n, sh, dt in [
            ("diagT", [P, H * CH], BF16), ("qdecT", [P, H * CH], BF16),
            ("kdec", [P, H], F32), ("kdecF", [P, H * NCHK], F32),
            ("rotm", [P, P], BF16), ("ident", [P, P], BF16),
            ("onesb", [P, 1], BF16), ("onesf", [1, P], F32),
            ("consts", [P, 4], F32)]:
        TB[n] = sbt("t_" + n, sh, dt)

    PS = ctx.enter_context(nc.psum_tensor("PS", [P, 4096], F32))

    def bank(i, w=512):
        return PS[:, i * 512:i * 512 + w]

    def bank_bf(i):
        return PS[:, i * 512:(i + 1) * 512].bitcast(BF16)

    sm_t1 = SM[:, 0:512]
    sm_sq = [SM[:, 512:768].bitcast(BF16), SM[:, 768:1024].bitcast(BF16)]
    sm_ktmp = SM[:, 1024:1280].bitcast(BF16)
    sm_sT = [SM[:, 1280:1536].bitcast(BF16), SM[:, 1536:1792].bitcast(BF16)]
    sm_qp = [SM[:, 1792:2048].bitcast(BF16), SM[:, 2048:2304].bitcast(BF16)]
    sm_kd = [SM[:, 512:768].bitcast(BF16), SM[:, 768:1024].bitcast(BF16)]
    sm_ev = [SM[:, 1280:1536].bitcast(BF16), SM[:, 1536:1792].bitcast(BF16)]
    sm_mu = ROWS[0:1, 0:512]
    sm_ms = ROWS[0:1, 512:1024]
    sm_va = ROWS[0:1, 1024:1536]
    sm_ri = ROWS[0:1, 512:1024]
    d_va = ROWS[0:1, 2048:2560]
    d_ri = ROWS[0:1, 2560:3072]

    sems = {}
    for name in ["dLD", "dLDs", "dEV", "dST",
                 "aLDt", "oLDg", "pLDs", "bLDs",
                 "aLD", "aPE", "aEV", "aSC", "aST",
                 "pLD", "pPE", "pEV",
                 "bLD", "bPE", "bEV", "bSC", "bST",
                 "oLD", "oPE", "oEV", "oST",
                 "cLD", "cPE", "cEV", "cSC", "cST",
                 "fLD", "fPE", "fEV", "fSC", "fST"]:
        sems[name] = Sem(ctx.enter_context(nc.semaphore(name)))
    S = sems
    pr = Prog()
    bguard = {}

    def OP(eng, make, sem=None, inc=1):
        if sem is not None:
            sem.n += inc
            h, v = sem.h, inc
            pr.em(eng, lambda e: make(e).then_inc(h, v))
            return sem.n
        pr.em(eng, lambda e: make(e))
        return None

    def guard_bank(i):
        if i in bguard:
            sem, val = bguard[i]
            pr.wait("tensor", sem, val)

    def set_guard(i, sem):
        bguard[i] = (sem, sem.n)

    def emit_tables():
        for n, src in [("diagT", diagT), ("qdecT", qdecT),
                       ("kdec", kdec), ("kdecF", kdecF), ("rotm", rotm),
                       ("ident", ident), ("onesb", onesb), ("onesf", onesf)]:
            OP("sync", lambda e, n=n, src=src: e.dma_start(
                out=TB[n][:], in_=src), S["aLDt"], 16)

    def emit_wprep():
        srcs = [(wqT, wqb, H * DH), (wkT, wkb, HKV * DH),
                (wvT, wvb, HKV * DH), (woT, wob, D)]
        buf = [A1[:].bitcast(F32), A2[:].bitcast(F32)]
        ob = [B1[:, 0:4096], B1[:, 4096:8192]]
        i = 0
        for srcT, dstB, ncols in srcs:
            rows = srcT.shape[0]
            for r0 in range(0, rows, P):
                for c0 in range(0, ncols, 2048):
                    w = min(2048, ncols - c0)
                    par = i % 2
                    if i >= 2:
                        pr.wait("sync", S["dEV"], wp_ev[i - 2])
                    OP("sync", lambda e, srcT=srcT, r0=r0, c0=c0, w=w,
                       par=par: e.dma_start(
                           out=buf[par][:, 0:w],
                           in_=srcT[r0:r0 + P, c0:c0 + w]), S["dLDs"], 16)
                    ldv = S["dLDs"].n
                    pr.wait("vector", S["dLDs"], ldv)
                    if i >= 2:
                        pr.wait("vector", S["dST"], wp_st[i - 2])
                    OP("vector", lambda e, par=par, w=w: e.tensor_copy(
                        ob[par][:, 0:w], buf[par][:, 0:w]), S["dEV"])
                    wp_ev.append(S["dEV"].n)
                    evv = S["dEV"].n
                    pr.wait("scalar", S["dEV"], evv)
                    OP("scalar", lambda e, dstB=dstB, r0=r0, c0=c0, w=w,
                       par=par: e.dma_start(
                           out=dstB[r0:r0 + P, c0:c0 + w],
                           in_=ob[par][:, 0:w]), S["dST"], 16)
                    wp_st.append(S["dST"].n)
                    i += 1
        pr.wait("sync", S["dEV"])
        pr.wait("vector", S["dST"])
        pr.wait("gpsimd", S["dST"])

    wp_ev = []
    wp_st = []

    def emit_decode():
        pk_t = A1[:].bitcast(I32)
        sc_raw = A2[:].bitcast(F16)
        sc_t = B1[:]
        wd_t = HB[:]

        def dec_one(n_i32, ld_pack, ld_scale, store):
            ld_pack()
            ld_scale()
            pr.wait("vector", S["dLDs"])
            pr.wait("vector", S["dLD"])
            W = 2 * n_i32
            OP("vector", lambda e, W=W: e.tensor_scalar(
                sc_t[:, 0:W], sc_raw[:, 0:W], float(2.0 ** 126), None,
                ALU.mult))
            pk16 = pk_t[:, 0:n_i32].bitcast(I16)
            xn16 = XN[:].bitcast(I16)
            for jp in range(4):
                sa = xn16[:, 0:W]
                sb_ = xn16[:, 8192:8192 + W]
                ss, ms = _SIGN_SH[jp], _MAG_SH[jp]
                if ss > 0:
                    OP("vector", lambda e, sa=sa, pk16=pk16, ss=ss:
                       e.tensor_scalar(sa, pk16, ss, 0x8000 - 0x10000,
                                       ALU.logical_shift_left,
                                       ALU.bitwise_and))
                else:
                    OP("vector", lambda e, sa=sa, pk16=pk16:
                       e.tensor_scalar(sa, pk16, 0x8000 - 0x10000, None,
                                       ALU.bitwise_and))
                if ms > 0:
                    OP("vector", lambda e, sb_=sb_, pk16=pk16, ms=ms:
                       e.tensor_scalar(sb_, pk16, ms, 0x01C0,
                                       ALU.logical_shift_left,
                                       ALU.bitwise_and))
                else:
                    OP("vector", lambda e, sb_=sb_, pk16=pk16, ms=ms:
                       e.tensor_scalar(sb_, pk16, -ms, 0x01C0,
                                       ALU.logical_shift_right,
                                       ALU.bitwise_and))
                wdj = wd_t[:, jp * W:(jp + 1) * W]
                OP("vector", lambda e, wdj=wdj, sa=sa, sb_=sb_:
                   e.tensor_tensor(wdj.bitcast(I16), sa, sb_, ALU.bitwise_or))
                OP("vector", lambda e, wdj=wdj, W=W:
                   e.tensor_tensor(wdj, wdj, sc_t[:, 0:W], ALU.mult),
                   S["dEV"])
                pr.wait("scalar", S["dEV"])
                store(jp, wdj)
            pr.wait("sync", S["dEV"])
            pr.wait("gpsimd", S["dEV"])
            pr.wait("vector", S["dST"])

        def dec_pair(n_i32, ld_pack, ld_scale):
            ld_pack()
            ld_scale()
            pr.wait("vector", S["dLDs"])
            pr.wait("vector", S["dLD"])
            W = 2 * n_i32
            OP("vector", lambda e, W=W: e.tensor_scalar(
                sc_t[:, 0:W], sc_raw[:, 0:W], float(2.0 ** 126), None,
                ALU.mult))
            xn16 = XN[:].bitcast(I16)
            for m in range(4):
                sa = xn16[:, 0:W]
                sb_ = xn16[:, 8192:8192 + W]
                for half, jj in ((0, 2 * m), (1, 2 * m + 1)):
                    pk16h = pk_t[half * 64:(half + 1) * 64,
                                 0:n_i32].bitcast(I16)
                    sah = sa[half * 64:(half + 1) * 64, :]
                    sbh = sb_[half * 64:(half + 1) * 64, :]
                    jp = jj % 4
                    ss, ms = _SIGN_SH[jp], _MAG_SH[jp]
                    if ss > 0:
                        OP("vector", lambda e, sah=sah, pk16h=pk16h, ss=ss:
                           e.tensor_scalar(sah, pk16h, ss, 0x8000 - 0x10000,
                                           ALU.logical_shift_left,
                                           ALU.bitwise_and))
                    else:
                        OP("vector", lambda e, sah=sah, pk16h=pk16h:
                           e.tensor_scalar(sah, pk16h, 0x8000 - 0x10000,
                                           None, ALU.bitwise_and))
                    if ms > 0:
                        OP("vector", lambda e, sbh=sbh, pk16h=pk16h, ms=ms:
                           e.tensor_scalar(sbh, pk16h, ms, 0x01C0,
                                           ALU.logical_shift_left,
                                           ALU.bitwise_and))
                    else:
                        OP("vector", lambda e, sbh=sbh, pk16h=pk16h, ms=ms:
                           e.tensor_scalar(sbh, pk16h, -ms, 0x01C0,
                                           ALU.logical_shift_right,
                                           ALU.bitwise_and))
                wdj = wd_t[:, m * W:(m + 1) * W]
                OP("vector", lambda e, wdj=wdj, sa=sa, sb_=sb_:
                   e.tensor_tensor(wdj.bitcast(I16), sa, sb_, ALU.bitwise_or))
                OP("vector", lambda e, wdj=wdj, W=W:
                   e.tensor_tensor(wdj, wdj, sc_t[:, 0:W], ALU.mult),
                   S["dEV"])
                pr.wait("scalar", S["dEV"])
                OP("scalar", lambda e, m=m, wdj=wdj: e.dma_start(
                    out=wdec_dn[(20 + m) * P:(20 + m + 1) * P, :],
                    in_=wdj), S["dST"], 16)
            pr.wait("sync", S["dEV"])
            pr.wait("gpsimd", S["dEV"])
            pr.wait("vector", S["dST"])

        GW = 2816
        for cc in range(2):
            for g in range(4):
                dec_one(
                    GW,
                    lambda cc=cc, g=g: OP("sync", lambda e: e.dma_start(
                        out=pk_t[:, 0:GW],
                        in_=gupk[cc * P:(cc + 1) * P, g * GW:(g + 1) * GW]),
                        S["dLDs"], 16),
                    lambda cc=cc, g=g: OP("gpsimd", lambda e: e.dma_start(
                        out=sc_raw[:, 0:2 * GW],
                        in_=gusc[cc * P:(cc + 1) * P,
                                 g * 2 * GW:(g + 1) * 2 * GW]),
                        S["dLD"], 16),
                    lambda jp, wdj, cc=cc, g=g: OP(
                        "scalar", lambda e, jp=jp, wdj=wdj: e.dma_start(
                            out=wdec_gu[(cc * 4 + jp) * P:
                                        (cc * 4 + jp + 1) * P,
                                        g * 2 * GW:(g + 1) * 2 * GW],
                            in_=wdj), S["dST"], 16))
        DW = 2048
        for cc2 in range(6):
            if cc2 < 5:
                def ldp(cc2=cc2):
                    OP("sync", lambda e: e.dma_start(
                        out=pk_t[:, 0:DW],
                        in_=dnpk[cc2 * P:(cc2 + 1) * P, :]), S["dLDs"], 16)

                def lds(cc2=cc2):
                    OP("gpsimd", lambda e: e.dma_start(
                        out=sc_raw[:, 0:2 * DW],
                        in_=dnsc[cc2 * P:(cc2 + 1) * P, :]), S["dLD"], 16)
            else:
                def ldp():
                    OP("sync", lambda e: e.dma_start(
                        out=pk_t[0:64, 0:DW], in_=dnpk[640:704, :]),
                        S["dLDs"], 16)
                    OP("sync", lambda e: e.dma_start(
                        out=pk_t[64:128, 0:DW], in_=dnpk[640:704, :]),
                        S["dLDs"], 16)

                def lds():
                    OP("gpsimd", lambda e: e.dma_start(
                        out=sc_raw[0:64, 0:2 * DW], in_=dnsc[640:704, :]),
                        S["dLD"], 16)
                    OP("gpsimd", lambda e: e.dma_start(
                        out=sc_raw[64:128, 0:2 * DW], in_=dnsc[640:704, :]),
                        S["dLD"], 16)
            if cc2 < 5:
                dec_one(
                    DW, ldp, lds,
                    lambda jp, wdj, cc2=cc2: OP(
                        "scalar", lambda e, jp=jp, wdj=wdj: e.dma_start(
                            out=wdec_dn[(cc2 * 4 + jp) * P:
                                        (cc2 * 4 + jp + 1) * P, :],
                            in_=wdj), S["dST"], 16))
            else:
                dec_pair(DW, ldp, lds)

    def emit_ln_tile(tt, src_dram, xn_out, ldS, peS, evS, scS, lnmile=None):
        xa = (A1 if tt % 2 == 0 else A2)[:]
        if lnmile is not None and len(lnmile) >= 2:
            pr.wait("gpsimd", evS, lnmile[-2])
        OP("gpsimd", lambda e, xa=xa, tt=tt: e.dma_start(
            out=xa.rearrange("p (k t) -> p k t", k=NKD),
            in_=src_dram[:, tt * TT:(tt + 1) * TT]
            .rearrange("(k p) t -> p k t", p=P)), ldS, 16)
        ldv = ldS.n
        pr.wait("scalar", ldS, ldv)
        guard_bank(6)
        guard_bank(7)
        pe_base = peS.n
        pr.wait("tensor", ldS, ldv)
        for k in range(NKD):
            OP("tensor", lambda e, xa=xa, k=k: e.matmul(
                bank(6, TT)[0:1, :], TB["onesb"][:],
                xa[:, k * TT:(k + 1) * TT],
                start=(k == 0), stop=(k == NKD - 1)),
                peS if k == NKD - 1 else None)
        for k in range(NKD):
            sq = sm_sq[k % 2]
            if k >= 2:
                pr.wait("scalar", peS, pe_base + 1 + (k - 1))
            OP("scalar", lambda e, xa=xa, k=k, sq=sq: e.activation(
                sq[:, 0:TT], xa[:, k * TT:(k + 1) * TT], AF.Square), scS)
            scv = scS.n
            pr.wait("tensor", scS, scv)
            OP("tensor", lambda e, k=k, sq=sq: e.matmul(
                bank(7, TT)[0:1, :], TB["onesb"][:], sq[:, 0:TT],
                start=(k == 0), stop=(k == NKD - 1)), peS)
        pev = peS.n
        pr.wait("vector", peS, pev)
        OP("vector", lambda e: e.tensor_scalar(
            sm_mu, bank(6, TT)[0:1, :], 1.0 / D, None, ALU.mult))
        OP("vector", lambda e: e.tensor_scalar(
            sm_ms, bank(7, TT)[0:1, :], 1.0 / D, None, ALU.mult))
        OP("vector", lambda e: e.tensor_tensor(
            sm_va, sm_mu, sm_mu, ALU.mult))
        OP("vector", lambda e: e.tensor_tensor(
            sm_va, sm_ms, sm_va, ALU.subtract))
        OP("vector", lambda e: e.tensor_scalar(
            sm_va, sm_va, 1e-5, None, ALU.add), evS)
        evv = evS.n
        pr.wait("scalar", evS, evv)
        OP("scalar", lambda e: e.activation(sm_ri, sm_va, AF.Ln), scS)
        scln = scS.n
        pr.wait("vector", scS, scln)
        OP("vector", lambda e: e.tensor_scalar(
            sm_ri, sm_ri, -0.5, None, ALU.mult), evS)
        evln = evS.n
        pr.wait("scalar", evS, evln)
        OP("scalar", lambda e: e.activation(sm_ri, sm_ri, AF.Exp), scS)
        scv2 = scS.n
        pr.wait("tensor", evS, evv)
        pr.wait("tensor", scS, scv2)
        OP("tensor", lambda e: e.matmul(
            bank(6, TT), TB["onesf"][:], sm_mu, start=True, stop=True))
        OP("tensor", lambda e: e.matmul(
            bank(7, TT), TB["onesf"][:], sm_ri, start=True, stop=True), peS)
        pev2 = peS.n
        pr.wait("vector", peS, pev2)
        for k in range(NKD):
            OP("vector", lambda e, xa=xa, k=k: e.tensor_tensor(
                sm_t1[:, 0:TT], xa[:, k * TT:(k + 1) * TT], bank(6, TT),
                ALU.subtract))
            OP("vector", lambda e, k=k, tt=tt: e.tensor_tensor(
                xn_out(k, tt), sm_t1[:, 0:TT], bank(7, TT), ALU.mult),
                evS if k == NKD - 1 else None)
        set_guard(6, evS)
        set_guard(7, evS)
        if lnmile is not None:
            lnmile.append(evS.n)
        else:
            pr.wait("gpsimd", evS)

    def emit_a():
        def xn_out(k, tt):
            return XN[:, k * T + tt * TT:k * T + (tt + 1) * TT]
        lnm = []
        for tt in range(NTT):
            emit_ln_tile(tt, xT, xn_out, S["aLD"], S["aPE"], S["aEV"],
                         S["aSC"], lnmile=lnm)
        pr.wait("gpsimd", S["aEV"])
        if debug:
            pr.wait("gpsimd", S["aEV"])
            OP("gpsimd", lambda e: e.dma_start(
                out=xn_dbg.rearrange("(k p) t -> p k t", p=P),
                in_=XN[:].rearrange("p (k t) -> p k t", k=NKD)),
                S["fST"], 16)
            OP("gpsimd", lambda e: e.dma_start(
                out=rows_dbg, in_=ROWS[:]), S["fST"], 16)

    def proj_tile(lhsT_fn, rhs_fn, out_sl, cos_sl, sin_sl, peS, evS, bk,
                  rotbank=6):
        guard_bank(bk)
        for k in range(NKD):
            OP("tensor", lambda e, k=k: e.matmul(
                bank(bk), lhsT_fn(k), rhs_fn(k),
                start=(k == 0), stop=(k == NKD - 1)),
                peS if k == NKD - 1 else None)
        pev = peS.n
        pr.wait("vector", peS, pev)
        OP("vector", lambda e: e.tensor_copy(out_sl, bank(bk)), evS)
        set_guard(bk, evS)
        if cos_sl is None:
            return
        evv = evS.n
        guard_bank(rotbank)
        pr.wait("tensor", evS, evv)
        OP("tensor", lambda e: e.matmul(
            bank(rotbank), TB["rotm"][:], out_sl, start=True, stop=True),
            peS)
        pev2 = peS.n
        pr.wait("vector", peS, pev2)
        OP("vector", lambda e: e.tensor_tensor(
            sm_t1[:, 0:TT], out_sl, cos_sl, ALU.mult))
        OP("vector", lambda e: e.scalar_tensor_tensor(
            sm_ktmp[:, 0:TT], bank(rotbank), 1.0, sin_sl,
            ALU.mult, ALU.mult))
        OP("vector", lambda e: e.tensor_tensor(
            out_sl, sm_t1[:, 0:TT], sm_ktmp[:, 0:TT], ALU.add), evS)
        set_guard(rotbank, evS)

    def transpose_tile(src_sl, dst_sl, pos, peS, scS):
        guard_bank(7)
        OP("tensor", lambda e: e.transpose(
            bank_bf(7)[:, pos * P:(pos + 1) * P], src_sl, TB["ident"][:]),
            peS)
        pev = peS.n
        pr.wait("scalar", peS, pev)
        OP("scalar", lambda e: e.copy(
            dst_sl, bank_bf(7)[:, pos * P:(pos + 1) * P]), scS)
        set_guard(7, scS)

    def emit_p():
        pr.wait("sync", S["dST"])
        cospf_sb = B1[:, 4096:6144]
        sinpf_sb = B1[:, 6144:8192]
        OP("gpsimd", lambda e: e.dma_start(out=cospf_sb, in_=cospf),
           S["pLD"], 16)
        OP("gpsimd", lambda e: e.dma_start(out=sinpf_sb, in_=sinpf),
           S["pLD"], 16)
        xnp = HB[:, 0:8192]
        kf = B1[:, 0:512]
        vf = B1[:, 512:1024]
        kt = B1[:, 1024:1536]
        vt = B1[:, 1536:2048]
        ksc = B1[:, 2048:2560]
        wk_sb = B2[:, 0:2048]
        wv_sb = B2[:, 2048:4096]

        def xnp_out(k, tt):
            return xnp[:, k * TT:(k + 1) * TT]

        for tt in range(NTT):
            emit_ln_tile(tt, xpT, xnp_out, S["pLD"], S["pPE"], S["pEV"],
                         S["aSC"])
            pr.wait("tensor", S["pEV"])
            for g_ in range(HKV):
                pr.wait("sync", S["pPE"])
                OP("sync", lambda e, g_=g_: e.dma_start(
                    out=wk_sb.rearrange("p (k m) -> p k m", k=NKD),
                    in_=wkb[:, g_ * DH:(g_ + 1) * DH]
                    .rearrange("(k p) m -> p k m", p=P)), S["pLDs"], 16)
                OP("sync", lambda e, g_=g_: e.dma_start(
                    out=wv_sb.rearrange("p (k m) -> p k m", k=NKD),
                    in_=wvb[:, g_ * DH:(g_ + 1) * DH]
                    .rearrange("(k p) m -> p k m", p=P)), S["pLDs"], 16)
                wld = S["pLDs"].n
                pr.wait("tensor", S["pLDs"], wld)
                proj_tile(lambda k: wk_sb[:, k * P:(k + 1) * P],
                          lambda k: xnp[:, k * TT:(k + 1) * TT],
                          kf,
                          cospf_sb[:, tt * TT:(tt + 1) * TT],
                          sinpf_sb[:, tt * TT:(tt + 1) * TT],
                          S["pPE"], S["pEV"], bk=0)
                proj_tile(lambda k: wv_sb[:, k * P:(k + 1) * P],
                          lambda k: xnp[:, k * TT:(k + 1) * TT],
                          vf, None, None, S["pPE"], S["pEV"], bk=1)
                for c4 in range(4):
                    tc = tt * 4 + c4
                    pr.wait("scalar", S["pPE"])
                    transpose_tile(kf[:, c4 * P:(c4 + 1) * P],
                                   kt[:, 0:P], 0, S["pPE"], S["bSC"])
                    transpose_tile(vf[:, c4 * P:(c4 + 1) * P],
                                   vt[:, 0:P], 1, S["pPE"], S["bSC"])
                    scv = S["bSC"].n
                    pr.wait("vector", S["bSC"], scv)
                    pr.wait("vector", S["pPE"])
                    for hi in range(4):
                        hh = 4 * g_ + hi
                        OP("vector", lambda e, hi=hi, hh=hh, tc=tc:
                           e.tensor_scalar(
                               ksc[:, hi * P:(hi + 1) * P], kt[:, 0:P],
                               TB["kdecF"][:, hh * NCHK + tc:
                                           hh * NCHK + tc + 1],
                               None, ALU.mult),
                           S["pEV"] if hi == 3 else None)
                    evv = S["pEV"].n
                    pr.wait("tensor", S["pEV"], evv)
                    for hi in range(4):
                        hh = 4 * g_ + hi
                        OP("tensor", lambda e, hi=hi, hh=hh, tc=tc: e.matmul(
                            PS[:, 1024 + hh * P:1024 + (hh + 1) * P],
                            ksc[:, hi * P:(hi + 1) * P], vt[:, 0:P],
                            start=(tc == 0), stop=(tc == NCHK - 1)),
                            S["pPE"] if hi == 3 else None)
        pr.wait("vector", S["pPE"])
        OP("vector", lambda e: e.tensor_copy(KV[:], PS[:, 1024:3072]))
        OP("vector", lambda e: e.tensor_copy(KVB[:], KV[:]), S["pEV"])
        if debug:
            pr.wait("gpsimd", S["pEV"])
            OP("gpsimd", lambda e: e.dma_start(out=kv_dbg, in_=KV[:]),
               S["fST"], 16)

    def emit_b():
        cosf_sb = HB[:, 16384:18432]
        sinf_sb = HB[:, 18432:20480]
        OP("gpsimd", lambda e: e.dma_start(out=cosf_sb, in_=cosf),
           S["bLD"], 16)
        OP("gpsimd", lambda e: e.dma_start(out=sinf_sb, in_=sinf),
           S["bLD"], 16)
        pr.wait("vector", S["bLD"])
        qg = B1[:]
        kf_full = B2[:, 0:2048]
        kt_full = B2[:, 2048:4096]
        og = HB[:, 0:8192]
        vf_full = HB[:, 8192:10240]
        vt_full = HB[:, 10240:12288]
        wkv = HB[:, 12288:16384]

        for g in range(NG):
            wq_sb = (A1 if g % 2 == 0 else A2)[:]
            if g >= 1:
                pr.wait("sync", S["bST"])
                pr.wait("sync", S["bEV"])
            OP("sync", lambda e, g=g, wq_sb=wq_sb: e.dma_start(
                out=wq_sb.rearrange("p (k m) -> p k m", k=NKD),
                in_=wqb[:, g * 512:(g + 1) * 512]
                .rearrange("(k p) m -> p k m", p=P)), S["bLDs"], 16)
            wqv = S["bLDs"].n
            OP("sync", lambda e, g=g: e.dma_start(
                out=wkv[:, 0:2048].rearrange("p (k m) -> p k m", k=NKD),
                in_=wkb[:, g * DH:(g + 1) * DH]
                .rearrange("(k p) m -> p k m", p=P)), S["bLDs"], 16)
            OP("sync", lambda e, g=g: e.dma_start(
                out=wkv[:, 2048:4096].rearrange("p (k m) -> p k m", k=NKD),
                in_=wvb[:, g * DH:(g + 1) * DH]
                .rearrange("(k p) m -> p k m", p=P)), S["bLDs"], 16)
            wkvv = S["bLDs"].n

            pr.wait("tensor", S["bLDs"], wqv)
            for ec in range(4):
                for t4 in range(NTT):
                    proj_tile(
                        lambda k, ec=ec, wq_sb=wq_sb: wq_sb[
                            :, k * 512 + ec * P:k * 512 + (ec + 1) * P],
                        lambda k, t4=t4: XN[:, k * T + t4 * TT:
                                            k * T + (t4 + 1) * TT],
                        qg[:, ec * T + t4 * TT:ec * T + (t4 + 1) * TT],
                        cosf_sb[:, t4 * TT:(t4 + 1) * TT],
                        sinf_sb[:, t4 * TT:(t4 + 1) * TT],
                        S["bPE"], S["bEV"], bk=(ec * NTT + t4) % 2)
                    if debug and g == 0 and ec == 0 and t4 == 0:
                        pr.wait("vector", S["bEV"])
                        OP("vector", lambda e: e.tensor_copy(
                            sm_ev[0][:, 0:TT], bank(0)), S["bEV"])
                        pr.wait("gpsimd", S["bEV"])
                        OP("gpsimd", lambda e: e.dma_start(
                            out=kv2_dbg[:, 0:512], in_=sm_ev[0][:, 0:TT]),
                            S["fST"], 16)
                        OP("gpsimd", lambda e: e.dma_start(
                            out=kv2_dbg[:, 512:1024], in_=qg[:, 0:TT]),
                            S["fST"], 16)
                        OP("gpsimd", lambda e: e.dma_start(
                            out=kv2_dbg[:, 1024:1536], in_=XN[:, 0:TT]),
                            S["fST"], 16)
                        OP("gpsimd", lambda e, wq_sb=wq_sb: e.dma_start(
                            out=kv2_dbg[:, 1536:2048], in_=wq_sb[:, 0:TT]),
                            S["fST"], 16)
            pr.wait("tensor", S["bLDs"], wkvv)
            for t4 in range(NTT):
                proj_tile(
                    lambda k: wkv[:, k * P:(k + 1) * P],
                    lambda k, t4=t4: XN[:, k * T + t4 * TT:
                                        k * T + (t4 + 1) * TT],
                    kf_full[:, t4 * TT:(t4 + 1) * TT],
                    cosf_sb[:, t4 * TT:(t4 + 1) * TT],
                    sinf_sb[:, t4 * TT:(t4 + 1) * TT],
                    S["bPE"], S["bEV"], bk=t4 % 2)
                proj_tile(
                    lambda k: wkv[:, 2048 + k * P:2048 + (k + 1) * P],
                    lambda k, t4=t4: XN[:, k * T + t4 * TT:
                                        k * T + (t4 + 1) * TT],
                    vf_full[:, t4 * TT:(t4 + 1) * TT],
                    None, None, S["bPE"], S["bEV"], bk=t4 % 2)
            for c in range(NCHK):
                transpose_tile(kf_full[:, c * P:(c + 1) * P],
                               kt_full[:, c * P:(c + 1) * P],
                               0, S["bPE"], S["bSC"])
                transpose_tile(vf_full[:, c * P:(c + 1) * P],
                               vt_full[:, c * P:(c + 1) * P],
                               1, S["bPE"], S["bSC"])
            trv = S["bSC"].n
            pr.wait("vector", S["bSC"], trv)
            pr.wait("tensor", S["bSC"], trv)
            if False and debug and g == 0:
                pr.wait("gpsimd", S["bEV"])
                pr.wait("gpsimd", S["bSC"])
                OP("gpsimd", lambda e: e.dma_start(out=qg_dbg, in_=qg),
                   S["fST"], 16)
                OP("gpsimd", lambda e: e.dma_start(
                    out=kv2_dbg[:, 0:2048], in_=kf_full), S["fST"], 16)
                OP("gpsimd", lambda e: e.dma_start(
                    out=kv2_dbg[:, 2048:4096], in_=vt_full), S["fST"], 16)
                OP("gpsimd", lambda e: e.dma_start(
                    out=kv2_dbg[:, 4096:6144], in_=KVB[:]), S["fST"], 16)
                OP("gpsimd", lambda e, wq_sb=wq_sb: e.dma_start(
                    out=wq_dbg, in_=wq_sb), S["fST"], 16)

            pe_mm = {}
            mask_ev = {}
            oev_sc = {}
            st_ev = {-1: S["bEV"].n}
            dsl = TB["diagT"][:, g * 512:(g + 1) * 512]
            qdsl = TB["qdecT"][:, g * 512:(g + 1) * 512]
            for c in range(NCHK):
                par = c % 2
                if c >= 2:
                    pr.wait("vector", S["bPE"], pe_mm[c - 2])
                qg3 = qg.rearrange("p (hh t) -> p hh t", hh=4)[
                    :, :, c * P:(c + 1) * P]
                qd3 = qdsl.rearrange("p (hh t) -> p hh t", hh=4)
                qp3 = sm_qp[par].rearrange("p (hh t) -> p hh t", hh=4)
                OP("vector", lambda e, qg3=qg3, qd3=qd3, qp3=qp3:
                   e.tensor_tensor(qp3, qg3, qd3, ALU.mult))
                for hi in range(4):
                    hh = 4 * g + hi
                    OP("vector", lambda e, hi=hi, hh=hh, c=c, par=par:
                       e.tensor_scalar(
                           sm_kd[par][:, hi * P:(hi + 1) * P],
                           kt_full[:, c * P:(c + 1) * P],
                           TB["kdec"][:, hh:hh + 1], None, ALU.mult),
                       S["bEV"] if hi == 3 else None)
                prep_ev = S["bEV"].n
                if c >= 2:
                    pr.wait("tensor", S["bEV"], mask_ev[c - 2])
                for hi in range(4):
                    OP("tensor", lambda e, hi=hi, c=c, par=par: e.matmul(
                        bank(par)[:, hi * P:(hi + 1) * P],
                        kf_full[:, c * P:(c + 1) * P],
                        qg[:, hi * T + c * P:hi * T + (c + 1) * P],
                        start=True, stop=True),
                        S["bPE"] if hi == 3 else None)
                pe_sc = S["bPE"].n
                pr.wait("vector", S["bPE"], pe_sc)
                OP("vector", lambda e, par=par, dsl=dsl: e.tensor_tensor(
                    sm_sT[par][:, 0:512], bank(par), dsl, ALU.mult),
                    S["bEV"])
                mask_ev[c] = S["bEV"].n
                pr.wait("tensor", S["bEV"], mask_ev[c])
                pr.wait("tensor", S["bEV"], prep_ev)
                pr.wait("tensor", S["bEV"], st_ev[c - 1])
                if c >= 2:
                    pr.wait("tensor", S["bSC"], oev_sc[c - 2])
                for hi in range(4):
                    hh = 4 * g + hi
                    OP("tensor", lambda e, hi=hi, c=c, par=par: e.matmul(
                        bank(2 + par)[:, hi * P:(hi + 1) * P],
                        vt_full[:, c * P:(c + 1) * P],
                        sm_sT[par][:, hi * P:(hi + 1) * P],
                        start=True, stop=False))
                    OP("tensor", lambda e, hi=hi, hh=hh, par=par: e.matmul(
                        bank(2 + par)[:, hi * P:(hi + 1) * P],
                        KVB[:, hh * P:(hh + 1) * P],
                        sm_qp[par][:, hi * P:(hi + 1) * P],
                        start=False, stop=True))
                for hi in range(4):
                    OP("tensor", lambda e, hi=hi, c=c, par=par: e.matmul(
                        bank(4 + par)[:, hi * P:(hi + 1) * P],
                        sm_kd[par][:, hi * P:(hi + 1) * P],
                        vt_full[:, c * P:(c + 1) * P],
                        start=True, stop=True),
                        S["bPE"] if hi == 3 else None)
                pe_mm[c] = S["bPE"].n
                pr.wait("scalar", S["bPE"], pe_mm[c])
                for hi in range(4):
                    OP("scalar", lambda e, hi=hi, c=c, par=par: e.copy(
                        og[:, hi * T + c * P:hi * T + (c + 1) * P],
                        bank(2 + par)[:, hi * P:(hi + 1) * P]),
                        S["bSC"] if hi == 3 else None)
                oev_sc[c] = S["bSC"].n
                pr.wait("vector", S["bPE"], pe_mm[c])
                for hi in range(4):
                    hh = 4 * g + hi
                    OP("vector", lambda e, hi=hi, hh=hh, par=par:
                       e.scalar_tensor_tensor(
                           KV[:, hh * P:(hh + 1) * P],
                           KV[:, hh * P:(hh + 1) * P], BLK[hh],
                           bank(4 + par)[:, hi * P:(hi + 1) * P],
                           ALU.mult, ALU.add))
                OP("vector", lambda e, g=g: e.tensor_copy(
                    KVB[:, g * 512:(g + 1) * 512],
                    KV[:, g * 512:(g + 1) * 512]), S["bEV"])
                st_ev[c] = S["bEV"].n
            pr.wait("gpsimd", S["bSC"], oev_sc[NCHK - 1])
            OP("gpsimd", lambda e, g=g: e.dma_start(
                out=o_bnc[g * 512:(g + 1) * 512, :]
                .rearrange("(hh p) t -> p hh t", p=P),
                in_=og.rearrange("p (hh t) -> p hh t", hh=4)), S["bST"], 16)
            set_guard(0, S["bEV"])
            set_guard(1, S["bEV"])
            set_guard(2, S["bSC"])
            set_guard(3, S["bSC"])
            set_guard(4, S["bEV"])
            set_guard(5, S["bEV"])

    def emit_op():
        pr.wait("sync", S["bST"])
        pr.wait("tensor", S["bEV"])
        pr.wait("tensor", S["bSC"])
        pr.wait("vector", S["bST"])
        pr.wait("scalar", S["bST"])
        pr.wait("gpsimd", S["bST"])
        omile = []
        osmile = []
        for t4 in range(NTT):
            ot = B1[:]
            if t4 >= 1:
                pr.wait("sync", S["oPE"])
            OP("sync", lambda e, t4=t4: e.dma_start(
                out=ot.rearrange("p (k t) -> p k t", k=NKD),
                in_=o_bnc[:, t4 * TT:(t4 + 1) * TT]
                .rearrange("(k p) t -> p k t", p=P)), S["oLD"], 16)
            otv = S["oLD"].n
            for idx, dc in enumerate(range(NKD)):
                slot = idx % 4
                wo_sb = (A1 if slot < 2 else A2)[
                    :, (slot % 2) * 4096:(slot % 2) * 4096 + 2048]
                if len(omile) >= 4:
                    pr.wait("gpsimd", S["oPE"], omile[-4])
                OP("gpsimd", lambda e, dc=dc, wo_sb=wo_sb: e.dma_start(
                    out=wo_sb.rearrange("p (k m) -> p k m", k=NKD),
                    in_=wob[:, dc * P:(dc + 1) * P]
                    .rearrange("(k p) m -> p k m", p=P)), S["oLDg"], 16)
                wov = S["oLDg"].n
                bk = dc % 2
                guard_bank(bk)
                pr.wait("tensor", S["oLD"], otv)
                pr.wait("tensor", S["oLDg"], wov)
                for k in range(NKD):
                    OP("tensor", lambda e, k=k, wo_sb=wo_sb, bk=bk: e.matmul(
                        bank(bk), wo_sb[:, k * P:(k + 1) * P],
                        ot[:, k * TT:(k + 1) * TT],
                        start=(k == 0), stop=(k == NKD - 1)),
                        S["oPE"] if k == NKD - 1 else None)
                pev = S["oPE"].n
                omile.append(pev)
                ev = sm_ev[0] if idx % 2 == 0 else sm_sq[0]
                pr.wait("vector", S["oPE"], pev)
                if len(osmile) >= 2:
                    pr.wait("vector", S["oST"], osmile[-2])
                OP("vector", lambda e, ev=ev, bk=bk: e.tensor_copy(
                    ev[:, 0:TT], bank(bk)), S["oEV"])
                set_guard(bk, S["oEV"])
                evv = S["oEV"].n
                pr.wait("scalar", S["oEV"], evv)
                OP("scalar", lambda e, ev=ev, dc=dc, t4=t4: e.dma_start(
                    out=at_bnc[dc * P:(dc + 1) * P, t4 * TT:(t4 + 1) * TT],
                    in_=ev[:, 0:TT]), S["oST"], 16)
                osmile.append(S["oST"].n)

    def emit_c(sub=3):
        pr.wait("sync", S["oST"])
        pr.wait("sync", S["oPE"])
        h_sb = HB[:]
        yt = B1[:]
        mile = []
        cgg = [0]
        for t4 in range(NTT):
            for up in ((0,) if sub == 1 else (0, 1)):
                for cg in range(11):
                    wg = (A1 if cgg[0] % 2 == 0 else A2)[:]
                    if len(mile) >= 2:
                        pr.wait("sync", S["cPE"], mile[-2])
                    base = 11264 * up + 1024 * cg
                    OP("sync", lambda e, base=base, wg=wg: e.dma_start(
                        out=wg.rearrange("p (b m) -> p b m", b=8),
                        in_=wdec_gu.rearrange("(b p) m -> p b m", p=P)[
                            :, :, base:base + 1024]), S["cLD"], 16)
                    wgv = S["cLD"].n
                    pr.wait("tensor", S["cLD"], wgv)
                    for ti in range(4):
                        t = 4 * cg + ti
                        bk = t % 2
                        guard_bank(bk)
                        for k in range(NKD):
                            b, par = _gu_block(k)
                            OP("tensor",
                               lambda e, k=k, b=b, par=par, ti=ti, t4=t4,
                               bk=bk, wg=wg: e.matmul(
                                   bank(bk),
                                   wg[:, b * 1024 + ti * 256 + par:
                                      b * 1024 + (ti + 1) * 256:2],
                                   XN[:, k * T + t4 * TT:
                                      k * T + (t4 + 1) * TT],
                                   start=(k == 0), stop=(k == NKD - 1)),
                               S["cPE"] if k == NKD - 1 else None)
                        pev = S["cPE"].n
                        if up == 0:
                            pr.wait("scalar", S["cPE"], pev)
                            OP("scalar", lambda e, t=t, bk=bk: e.activation(
                                h_sb[:, t * TT:(t + 1) * TT], bank(bk),
                                AF.Silu), S["cSC"])
                            set_guard(bk, S["cSC"])
                        else:
                            pr.wait("vector", S["cPE"], pev)
                            OP("vector", lambda e, t=t, bk=bk:
                               e.tensor_tensor(
                                   h_sb[:, t * TT:(t + 1) * TT],
                                   h_sb[:, t * TT:(t + 1) * TT],
                                   bank(bk), ALU.mult), S["cEV"])
                            set_guard(bk, S["cEV"])
                    mile.append(S["cPE"].n)
                    cgg[0] += 1
                    if False:
                        pass
            if sub < 3:
                continue
            pr.wait("tensor", S["cEV"])
            for dc in range(NKD):
                wdn = (A1 if cgg[0] % 2 == 0 else A2)[:, 0:6144]
                if len(mile) >= 2:
                    pr.wait("sync", S["cPE"], mile[-2])
                OP("sync", lambda e, dc=dc, wdn=wdn: e.dma_start(
                    out=wdn.rearrange("p (b m) -> p b m", b=24),
                    in_=wdec_dn.rearrange("(b p) m -> p b m", p=P)[
                        :, :, dc * 256:(dc + 1) * 256]), S["cLD"], 16)
                wdv = S["cLD"].n
                bk = dc % 2
                guard_bank(bk)
                pr.wait("tensor", S["cLD"], wdv)
                for t in range(NIC):
                    if t < 40:
                        b, par = _dn_block(t)
                    else:
                        b, par = 20 + (t - 40), (t - 40) // 2
                    OP("tensor", lambda e, t=t, b=b, par=par, wdn=wdn, bk=bk:
                       e.matmul(
                           bank(bk),
                           wdn[:, b * 256 + par:(b + 1) * 256:2],
                           h_sb[:, t * TT:(t + 1) * TT],
                           start=(t == 0), stop=(t == NIC - 1)),
                       S["cPE"] if t == NIC - 1 else None)
                pev = S["cPE"].n
                pr.wait("vector", S["cPE"], pev)
                if t4 > 0 and dc == 0:
                    pr.wait("vector", S["cST"])
                OP("vector", lambda e, dc=dc, bk=bk: e.tensor_copy(
                    yt[:, dc * TT:(dc + 1) * TT], bank(bk)), S["cEV"])
                set_guard(bk, S["cEV"])
                mile.append(S["cPE"].n)
                cgg[0] += 1
            evall = S["cEV"].n
            pr.wait("scalar", S["cEV"], evall)
            guard_bank(6)
            sc_base = S["cSC"].n
            pe_base6 = S["cPE"].n
            for dc in range(NKD):
                sq = sm_sq[dc % 2]
                if dc >= 2:
                    pr.wait("scalar", S["cPE"], pe_base6 + dc - 1)
                OP("scalar", lambda e, dc=dc, sq=sq: e.activation(
                    sq[:, 0:TT], yt[:, dc * TT:(dc + 1) * TT],
                    AF.Square), S["cSC"])
                scv = S["cSC"].n
                pr.wait("tensor", S["cSC"], scv)
                OP("tensor", lambda e, dc=dc, sq=sq: e.matmul(
                    bank(6, TT)[0:1, :], TB["onesb"][:], sq[:, 0:TT],
                    start=(dc == 0), stop=(dc == NKD - 1)), S["cPE"])
            pev = S["cPE"].n
            pr.wait("vector", S["cPE"], pev)
            OP("vector", lambda e, t4=t4: e.tensor_copy(
                MSQ[:, t4 * TT:(t4 + 1) * TT], bank(6, TT)[0:1, :]),
                S["cEV"])
            set_guard(6, S["cEV"])
            pr.wait("gpsimd", S["cEV"])
            OP("gpsimd", lambda e, t4=t4: e.dma_start(
                out=y_bnc[:, t4 * TT:(t4 + 1) * TT]
                .rearrange("(k p) t -> p k t", p=P),
                in_=yt.rearrange("p (k t) -> p k t", k=NKD)), S["cST"], 16)

    def emit_d():
        xf = XN[:].bitcast(F32)
        att = B1[:]
        yb = HB[:, 0:8192]
        for t4 in range(NTT):
            half_ld = xf[:, (t4 % 2) * 8192:(t4 % 2) * 8192 + 8192]
            if t4 >= 2:
                pr.wait("sync", S["fEV"])
            if t4 >= 1:
                pr.wait("sync", S["fST"])
            OP("sync", lambda e, half_ld=half_ld, t4=t4: e.dma_start(
                out=half_ld.rearrange("p (k t) -> p k t", k=NKD),
                in_=xT[:, t4 * TT:(t4 + 1) * TT]
                .rearrange("(k p) t -> p k t", p=P)), S["fLD"], 16)
            OP("sync", lambda e, t4=t4: e.dma_start(
                out=att.rearrange("p (k t) -> p k t", k=NKD),
                in_=at_bnc[:, t4 * TT:(t4 + 1) * TT]
                .rearrange("(k p) t -> p k t", p=P)), S["fLD"], 16)
            OP("sync", lambda e, t4=t4: e.dma_start(
                out=yb.rearrange("p (k t) -> p k t", k=NKD),
                in_=y_bnc[:, t4 * TT:(t4 + 1) * TT]
                .rearrange("(k p) t -> p k t", p=P)), S["fLD"], 16)
            ldv = S["fLD"].n
            OP("vector", lambda e, t4=t4: e.tensor_scalar(
                d_va, MSQ[:, t4 * TT:(t4 + 1) * TT], 1.0 / D, 1e-6,
                ALU.mult, ALU.add), S["fEV"])
            pr.wait("scalar", S["fEV"])
            OP("scalar", lambda e: e.activation(d_va, d_va, AF.Ln),
               S["fSC"])
            pr.wait("vector", S["fSC"])
            OP("vector", lambda e: e.tensor_scalar(
                d_ri, d_va, -0.5, None, ALU.mult), S["fEV"])
            pr.wait("scalar", S["fEV"])
            OP("scalar", lambda e: e.activation(d_ri, d_ri, AF.Exp),
               S["fSC"])
            scv = S["fSC"].n
            guard_bank(7)
            pr.wait("tensor", S["fSC"], scv)
            OP("tensor", lambda e: e.matmul(
                bank(7, TT), TB["onesf"][:], d_ri, start=True, stop=True),
                S["fPE"])
            pev = S["fPE"].n
            pr.wait("vector", S["fPE"], pev)
            pr.wait("vector", S["fLD"], ldv)
            for k in range(NKD):
                outh = (A1 if k < 8 else A2)[:].bitcast(F32)
                kk = k % 8
                OP("vector", lambda e, k=k, kk=kk, outh=outh,
                   half_ld=half_ld: e.tensor_tensor(
                       outh[:, kk * TT:(kk + 1) * TT],
                       half_ld[:, k * TT:(k + 1) * TT],
                       att[:, k * TT:(k + 1) * TT], ALU.add))
                OP("vector", lambda e, k=k: e.tensor_tensor(
                    sm_t1[:, 0:TT], yb[:, k * TT:(k + 1) * TT],
                    bank(7, TT), ALU.mult))
                OP("vector", lambda e, kk=kk, outh=outh: e.tensor_tensor(
                    outh[:, kk * TT:(kk + 1) * TT],
                    outh[:, kk * TT:(kk + 1) * TT],
                    sm_t1[:, 0:TT], ALU.add),
                    S["fEV"] if (k == 7 or k == NKD - 1) else None)
            set_guard(7, S["fEV"])
            ev1 = S["fEV"].n - 1
            ev2 = S["fEV"].n
            pr.wait("gpsimd", S["fEV"], ev1)
            OP("gpsimd", lambda e, t4=t4: e.dma_start(
                out=outT.rearrange("(k p) t -> p k t", p=P)[
                    :, 0:8, t4 * TT:(t4 + 1) * TT],
                in_=A1[:].bitcast(F32).rearrange("p (k t) -> p k t", k=8)),
                S["fST"], 16)
            pr.wait("gpsimd", S["fEV"], ev2)
            OP("gpsimd", lambda e, t4=t4: e.dma_start(
                out=outT.rearrange("(k p) t -> p k t", p=P)[
                    :, 8:16, t4 * TT:(t4 + 1) * TT],
                in_=A2[:].bitcast(F32).rearrange("p (k t) -> p k t", k=8)),
                S["fST"], 16)
            pr.wait("vector", S["fST"])

        pr.wait("gpsimd", S["fST"])

    emit_tables()
    emit_wprep()
    pr.wait("tensor", S["aLDt"])
    pr.wait("vector", S["aLDt"])
    pr.wait("scalar", S["aLDt"])
    if "dec" in phases:
        emit_decode()
    if "a" in phases:
        pr.wait("gpsimd", S["dEV"])
        pr.wait("gpsimd", S["dST"])
        pr.wait("vector", S["dST"])
        emit_a()
    if "p" in phases:
        emit_p()
    if "b" in phases:
        pr.wait("tensor", S["pEV"])
        emit_b()
    if "op" in phases:
        emit_op()
    if "c" in phases:
        emit_c()
    elif "c1" in phases:
        emit_c(sub=1)
    elif "c2" in phases:
        emit_c(sub=2)
    elif "c4" in phases:
        emit_c(sub=4)
    elif "c5" in phases:
        emit_c(sub=5)
    elif "c6" in phases:
        emit_c(sub=6)
    if "d" in phases:
        pr.wait("sync", S["cST"])
        pr.wait("sync", S["cEV"])
        emit_d()
    else:
        pr.wait("gpsimd", S["aEV"])
        OP("gpsimd", lambda e: e.dma_start(
            out=outT.rearrange("(k p) t -> p k t", p=P),
            in_=XN[:].rearrange("p (k t) -> p k t", k=NKD).bitcast(BF16))
            if False else e.dma_start(
            out=outT[0:P, :], in_=SM[:, 0:T]), S["fST"], 16)
        pr.wait("gpsimd", S["fST"])

    with nc.Block() as block:
        @block.sync
        def _(sync):
            pr.run(sync, "sync")

        @block.scalar
        def _(scalar):
            pr.run(scalar, "scalar")

        @block.vector
        def _(vector):
            pr.run(vector, "vector")

        @block.tensor
        def _(tensor):
            pr.run(tensor, "tensor")

        @block.gpsimd
        def _(gpsimd):
            pr.run(gpsimd, "gpsimd")

    ctx.close()
    return nc


def _tables_np():
    pd = _perm_d()
    pi = _perm_i()
    inv = 1.0 / 10000.0 ** (np.arange(0, DH, 2) / DH)
    th_all = np.arange(4096)[:, None] * inv[None, :]
    cos_all = np.cos(th_all)
    sin_all = np.sin(th_all)

    def featmaj(tbl, tok0):
        s = tbl[tok0:tok0 + T].T
        return np.concatenate([s, s], axis=0).astype(ml_dtypes.bfloat16)

    r = np.arange(CH)
    diagT = np.zeros((P, H * CH), np.float32)
    qdecT = np.zeros((P, H * CH), np.float32)
    kdec = np.zeros((P, H), np.float32)
    kdecF = np.zeros((P, H * NCHK), np.float32)
    for hh in range(H):
        dm = np.where(r[None, :] >= r[:, None],
                      np.exp(-LAM[hh] * (r[None, :] - r[:, None])), 0.0)
        diagT[:, hh * CH:(hh + 1) * CH] = dm
        qdecT[:, hh * CH:(hh + 1) * CH] = np.exp(
            -LAM[hh] * (r + 1.0))[None, :]
        kdec[:, hh] = np.exp(-LAM[hh] * (CH - 1.0 - r))
        for tc in range(NCHK):
            kdecF[:, hh * NCHK + tc] = np.exp(
                -LAM[hh] * (2047.0 - (CH * tc + r)))
    rotm = np.zeros((P, P), np.float32)
    for m in range(64):
        rotm[m + 64, m] = -1.0
        rotm[m, m + 64] = 1.0
    return dict(pd=pd, pi=pi, cos_all=cos_all, sin_all=sin_all,
                featmaj=featmaj, diagT=diagT.astype(ml_dtypes.bfloat16),
                qdecT=qdecT.astype(ml_dtypes.bfloat16), kdec=kdec,
                kdecF=kdecF, rotm=rotm.astype(ml_dtypes.bfloat16))


def stage_inputs(inputs):
    t = _tables_np()
    pd, pi = t["pd"], t["pi"]
    x = np.asarray(inputs["L0_attn_x"], np.float32)
    wq = np.asarray(inputs["L0_attn_wq"], np.float32)
    wk = np.asarray(inputs["L0_attn_wk"], np.float32)
    wv = np.asarray(inputs["L0_attn_wv"], np.float32)
    wo = np.asarray(inputs["L0_attn_wo"], np.float32)
    gup = np.asarray(inputs["L0_ffn_gate_up_packed"], np.int32)
    gus = np.asarray(inputs["L0_ffn_gate_up_scales"])
    dnp_ = np.asarray(inputs["L0_ffn_down_packed"], np.int32)
    dns = np.asarray(inputs["L0_ffn_down_scales"])

    row_perm = np.concatenate([pi, IDIM + pi])
    shared = {
        "wqT": np.ascontiguousarray(wq.T[pd]),
        "wkT": np.ascontiguousarray(wk.T[pd]),
        "wvT": np.ascontiguousarray(wv.T[pd]),
        "woT": np.ascontiguousarray(wo[pd].T),
        "gupk": np.ascontiguousarray(gup[row_perm].T),
        "gusc": np.ascontiguousarray(
            np.repeat(np.repeat(gus[:, row_perm], 16, axis=0), 2, axis=1)
            .astype(np.float16)),
        "dnpk": np.ascontiguousarray(dnp_[pd].T),
        "dnsc": np.ascontiguousarray(
            np.repeat(np.repeat(dns[:, pd], 16, axis=0), 2, axis=1)
            .astype(np.float16)),
        "diagT": t["diagT"], "qdecT": t["qdecT"],
        "kdec": t["kdec"], "kdecF": t["kdecF"],
        "rotm": t["rotm"],
        "ident": np.eye(P, dtype=ml_dtypes.bfloat16),
        "onesb": np.ones((P, 1), ml_dtypes.bfloat16),
        "onesf": np.ones((1, P), np.float32),
        "consts": np.broadcast_to(
            np.array([1e-5, 1e-6, -0.5, 1.0 / D], np.float32), (P, 4)
        ).copy(),
    }
    in_maps = []
    for c in range(8):
        b, hf = c // 2, c % 2
        tok0 = hf * T
        xc = np.ascontiguousarray(x[b, tok0:tok0 + T][:, pd].T)
        if hf == 1:
            xp = np.ascontiguousarray(x[b, 0:T][:, pd].T)
        else:
            xp = np.zeros((D, T), np.float32)
        m = dict(shared)
        m["xT"] = xc
        m["xpT"] = xp
        m["cosf"] = t["featmaj"](t["cos_all"], tok0)
        m["sinf"] = t["featmaj"](t["sin_all"], tok0)
        m["cospf"] = t["featmaj"](t["cos_all"], 0)
        m["sinpf"] = t["featmaj"](t["sin_all"], 0)
        in_maps.append(m)
    return in_maps, pd


def kernel(**inputs):
    in_maps, pd = stage_inputs(inputs)
    nc = build_nc(debug=False)
    res = run_bass_kernel_spmd(nc, in_maps, core_ids=list(range(8)))
    x = np.asarray(inputs["L0_attn_x"])
    out = np.empty((4, 4096, 2048), np.float32)
    for c in range(8):
        b, hf = c // 2, c % 2
        blk = out[b, hf * T:(hf + 1) * T]
        blk[:, pd] = res.results[c]["outT"].T
    return out
